# version 01
# baseline (speedup 1.0000x reference)
import numpy as np
from contextlib import ExitStack

import concourse.bass as bass
import concourse.mybir as mybir
from concourse.bass_utils import run_bass_kernel_spmd

F32 = mybir.dt.float32
BF16 = mybir.dt.bfloat16
FP16 = mybir.dt.float16
FP8 = mybir.dt.float8e4
I32 = mybir.dt.int32
AF = mybir.ActivationFunctionType
ALU = mybir.AluOpType
AX = mybir.AxisListType.X

NCORES = 8
B = 16
BPC = B // NCORES
T = 2048
D = 1024
N = BPC * T
H = 4
KD = 256
VD = 256
M = 64
CH = 128
NCH = T // CH
HID0 = 3072
HID1 = 4096
WKV_W = 128
NWIN = T // WKV_W
EPS_WKV = 1e-8
LN2 = 0.6931471805599453
LN16 = 2.772588722239781
GSA_SCALE = KD ** (-0.5)


class Plan:

    ENGINES = ("sync", "gpsimd", "vector", "scalar", "tensor")

    NPHYS = 88

    def __init__(self, nc):
        self.nc = nc
        self.ops = {e: [] for e in self.ENGINES}
        self.phys = []
        self.phys_count = []
        self.lmap = {}
        self.next_slot = 0
        self.dma_sems = {}
        self.nbar = 0

    def _slot(self, name):
        if name not in self.lmap:
            i = self.next_slot
            assert i < self.NPHYS, f"out of virtual sem slots in one phase ({name})"
            self.next_slot += 1
            while len(self.phys) <= i:
                self.phys.append(self.nc.ctx.enter_context(self.nc.semaphore(f"vs{len(self.phys)}")))
                self.phys_count.append(0)
            self.lmap[name] = (i, self.phys_count[i])
        return self.lmap[name]

    def inc(self, name, k):
        i, base = self._slot(name)
        self.phys_count[i] += k
        return self.phys_count[i] - base

    def val(self, name):
        i, base = self._slot(name)
        return self.phys_count[i] - base

    def add(self, engine, fn):
        self.ops[engine].append(fn)

    def wait(self, engine, name, value=None):
        i, base = self._slot(name)
        v = (self.phys_count[i] - base) if value is None else value
        if v <= 0:
            return
        s = self.phys[i]
        pv = base + v
        self.add(engine, lambda eng, s=s, pv=pv: eng.wait_ge(s, pv))

    def run(self, engine, fn, inc=None, inck=1):
        if inc is None:
            self.add(engine, fn)
            return None
        v = self.inc(inc, inck)
        i, base = self.lmap[inc]
        s = self.phys[i]
        self.add(engine, lambda eng, fn=fn, s=s, k=inck: fn(eng).then_inc(s, k))
        return v

    def dma(self, engine, out, in_, inc, transpose=False):
        v = self.inc(inc, 16)
        i, base = self.lmap[inc]
        s = self.phys[i]
        self.dma_sems[inc] = engine
        if transpose:
            self.add(engine, lambda eng, o=out, i_=in_, s=s: eng.dma_start(out=o, in_=i_, transpose=True).then_inc(s, 16))
        else:
            self.add(engine, lambda eng, o=out, i_=in_, s=s: eng.dma_start(out=o, in_=i_).then_inc(s, 16))
        return v

    def vspacer(self):
        self.add("vector", lambda v: v.tensor_copy(_SB["vspc"][:, 0:8], _SB["vspc"][:, 8:16]))

    def barrier(self, engines=("sync", "gpsimd", "vector", "scalar", "tensor")):
        for name, engine in list(self.dma_sems.items()):
            self.wait(engine, name, self.val(name))
        self.dma_sems = {}
        self.nbar += 1
        bar = f"bar{self.nbar}"
        for e in engines:
            self.run(e, lambda eng: eng.nop(), inc=bar)
        tot = self.val(bar)
        for e in engines:
            self.wait(e, bar, tot)
        self.lmap = {}
        self.next_slot = 0

    def retire_all_dmas(self):
        for name, engine in self.dma_sems.items():
            self.wait(engine, name, self.val(name))

    def emit(self, blk):
        for name in self.ENGINES:
            ops = self.ops[name]
            getattr(blk, name)(lambda eng, ops=ops: [f(eng) for f in ops])


class Banks:

    class _UsesView:
        def __init__(self, plan):
            self.plan = plan

        def get(self, bankname, default=0):
            return self.plan.val(f"{bankname}_ev")

    def __init__(self, plan):
        self.plan = plan
        self.uses = Banks._UsesView(plan)

    def use(self, bankname, mm_fns, evac_fns, evac_engine="vector", extra_mm_wait=None):
        p = self.plan
        u = p.val(f"{bankname}_ev")
        if u > 0:
            p.wait("tensor", f"{bankname}_ev", u)
        if extra_mm_wait:
            for nm, v in extra_mm_wait:
                p.wait("tensor", nm, v)
        for i, fn in enumerate(mm_fns):
            p.run("tensor", fn, inc=f"{bankname}_mm" if i == len(mm_fns) - 1 else None)
        p.wait(evac_engine, f"{bankname}_mm", u + 1)
        for i, fn in enumerate(evac_fns):
            p.run(evac_engine, fn, inc=f"{bankname}_ev" if i == len(evac_fns) - 1 else None)


def ts2(i, sz):
    return slice(i * sz, (i + 1) * sz)


_SB = {}
_CCOL_VALUES = [1e-5, -LN16]
_CCOL = {}


def ccol(v):
    return _CCOL[v]


def build_program(debug=False):
    nc = bass.Bass("TRN2", target_bir_lowering=False, detect_race_conditions=False)
    g = {}

    def dram_in(name, shape, dtype):
        g[name] = nc.dram_tensor(name, list(shape), dtype, kind="ExternalInput").ap()

    def dram_scratch(name, shape, dtype, out=False):
        kind = "ExternalOutput" if (out or debug) else "Internal"
        g[name] = nc.dram_tensor(name, list(shape), dtype, kind=kind).ap()

    dram_in("x", (N, D), F32)
    dram_in("Wq", (D, D), F32)
    dram_in("Wk", (D, D), F32)
    dram_in("Wv", (D, D), F32)
    dram_in("Wf", (D, H * M), F32)
    dram_in("gnw", (VD,), F32)
    dram_in("Wo", (D, D), F32)
    dram_in("wg", (HID0, D), I32)
    dram_in("wu", (HID0, D), I32)
    dram_in("wd", (D, HID0), I32)
    dram_in("t_first", (D,), F32)
    dram_in("Wk_t", (D, D), F32)
    dram_in("Wv_t", (D, D), F32)
    dram_in("Wr_t", (D, D), F32)
    dram_in("Wk_c", (HID1, D), F32)
    dram_in("Wv_c", (D, HID1), F32)
    dram_in("Wr_c", (D, D), F32)
    for nm in ("ln1_g", "ln1_b", "ln2_g", "ln2_b", "mix_k", "mix_v", "mix_r",
               "cmix_k", "cmix_r", "b2"):
        dram_in(nm, (D,), F32)
    dram_in("w1", (HID1, D), F32)
    dram_in("b1", (HID1,), F32)
    dram_in("w2", (D, HID1), F32)
    dram_in("eye", (128, 128), BF16)
    dram_in("triu", (128, 128), BF16)
    dram_in("mask_fm", (D, N), FP16)

    dram_scratch("out", (N, D), F32, out=True)
    dram_scratch("xn_sp", (D, N), BF16)
    dram_scratch("s0_sp", (N, D), BF16)
    dram_scratch("h_sp", (HID0, N), BF16)
    dram_scratch("x1_sp", (N, D), F32)
    dram_scratch("xn1_sp", (N, D), F32)
    dram_scratch("xa_sp", (N, D), F32)
    dram_scratch("g1_sp", (N, D), BF16)
    if debug:
        for nm, w, dt in (("dbg_qfm", 2048, BF16), ("dbg_kfm", 2048, BF16), ("dbg_vtm", 2048, BF16),
                          ("dbg_lsg", 512, F32), ("dbg_cum", 512, F32), ("dbg_SAfm", 512, BF16),
                          ("dbg_SAtm", 512, BF16), ("dbg_SBtm", 512, BF16), ("dbg_LG", 512, F32),
                          ("dbg_on", 2048, BF16), ("dbg_HK", 1024, BF16), ("dbg_HV", 1024, BF16),
                          ("dbg_Dtm", 512, F32), ("dbg_PTT", 1024, BF16), ("dbg_emA", 512, F32), ("dbg_emB", 512, F32), ("dbg_atr", 128, BF16), ("dbg_hkt", 128, BF16), ("dbg_lgp", 64, F32)):
            dram_scratch(nm, (128, w), dt)

    plan = Plan(nc)
    global _SB
    _SB = {}
    with ExitStack() as top:
        TE = top.enter_context

        def sbuf(name, shape, dtype):
            _SB[name] = TE(nc.sbuf_tensor(name, list(shape), dtype))
            return _SB[name]

        psf = [TE(nc.psum_tensor(f"psf{i}", [128, 512], F32)) for i in range(6)]
        psb = [TE(nc.psum_tensor(f"psb{i}", [128, 1024], BF16)) for i in range(2)]
        banks = Banks(plan)

        p0_consts(nc, plan, g, sbuf)

        with ExitStack() as mid:
            _SB["xn_fm"] = mid.enter_context(nc.sbuf_tensor("xn_fm", [128, 8 * N], BF16))
            with ExitStack() as ph:
                def psbuf(name, shape, dtype):
                    _SB[name] = ph.enter_context(nc.sbuf_tensor(name, list(shape), dtype))
                    return _SB[name]
                p1_ln0(nc, plan, g, sbuf, psbuf, psf, psb)
                plan.barrier()
            with ExitStack() as ph:
                def psbuf(name, shape, dtype):
                    _SB[name] = ph.enter_context(nc.sbuf_tensor(name, list(shape), dtype))
                    return _SB[name]
                p2_gsa(nc, plan, g, sbuf, psbuf, psf, psb, banks, debug)
                plan.barrier()
        for fn in (p3a_ffn0, p3b_ffn0, p4_wkv, p5_ffn1, p6_cmix):
            with ExitStack() as ph:
                def psbuf(name, shape, dtype):
                    _SB[name] = ph.enter_context(nc.sbuf_tensor(name, list(shape), dtype))
                    return _SB[name]
                fn(nc, plan, g, sbuf, psbuf, psf, psb, banks)
                plan.barrier()
        plan.retire_all_dmas()

        blk = TE(nc.Block())
        plan.emit(blk)
    return nc




def p0_consts(nc, plan, g, sbuf):
    sbuf("vspc", (128, 16), F32)
    plan.run("vector", lambda v: v.memset(_SB["vspc"][:], 0.0))
    sbuf("eye_s", (128, 128), BF16)
    sbuf("triu_s", (128, 128), BF16)
    sbuf("zeros_s", (128, 128), F32)
    plan.dma("sync", _SB["eye_s"][:], g["eye"], "c_ld")
    plan.dma("sync", _SB["triu_s"][:], g["triu"], "c_ld")
    plan.run("vector", lambda v: v.memset(_SB["zeros_s"][:], 0.0), inc="c_set")
    sbuf("warm_s", (128, 8), F32)
    plan.run("vector", lambda v: v.memset(_SB["warm_s"][:, 0:4], 1.0), inc="c_set")
    plan.wait("scalar", "c_set", plan.val("c_set"))
    for fn in (AF.Sqrt, AF.Exp, AF.Ln, AF.Sigmoid, AF.Silu, AF.Abs):
        plan.run("scalar", lambda s, fn=fn: s.activation(_SB["warm_s"][:, 4:8], _SB["warm_s"][:, 0:4], fn), inc="c_warm")
    cc = sbuf("ccol_s", (128, max(1, len(_CCOL_VALUES))), F32)
    for i, v in enumerate(_CCOL_VALUES):
        _CCOL[v] = cc[:, i:i + 1]
        plan.run("vector", lambda ve, i=i, v=v: ve.memset(_SB["ccol_s"][:, i:i + 1], v), inc="c_set")
    for e in ("scalar", "tensor", "vector"):
        plan.wait(e, "c_set")
    plan.wait("scalar", "c_warm", plan.val("c_warm"))


def p1_ln0(nc, plan, g, sbuf, psbuf, psf, psb):
    x = g["x"]
    psbuf("p1_xt", (128, 2 * D), F32)
    psbuf("p1_xnt", (128, 2 * D), BF16)
    psbuf("p1_scr", (128, D), F32)
    psbuf("p1_st", (128, 16), F32)
    ntile = N // 128

    for i in range(ntile):
        bi = i % 2
        o = 8 * bi
        if i >= 2:
            plan.wait("sync", "p1_xtfree", i - 1)
        plan.dma("sync", _SB["p1_xt"][:, ts2(bi, D)], x[ts2(i, 128), :], f"p1_ld{bi}")
        plan.wait("vector", f"p1_ld{bi}", (i // 2 + 1) * 16)
        xv = _SB["p1_xt"][:, ts2(bi, D)]
        st = _SB["p1_st"]
        if i >= 2:
            plan.wait("vector", "p1_sq", i - 1)
        plan.run("vector", lambda v, xv=xv, st=st, o=o: v.reduce_sum(st[:, o:o + 1], xv, axis=AX))
        plan.run("vector", lambda v, xv=xv: v.tensor_tensor(_SB["p1_scr"][:], xv, xv, ALU.mult))
        plan.vspacer()
        plan.run("vector", lambda v, st=st, o=o: v.reduce_sum(st[:, o + 1:o + 2], _SB["p1_scr"][:], axis=AX))
        plan.run("vector", lambda v, st=st, o=o: v.tensor_scalar(st[:, o + 2:o + 3], st[:, o:o + 1], 1.0 / D, None, ALU.mult))
        plan.vspacer()
        plan.run("vector", lambda v, st=st, o=o: v.tensor_tensor(st[:, o + 3:o + 4], st[:, o + 2:o + 3], st[:, o + 2:o + 3], ALU.mult))
        plan.vspacer()
        plan.run("vector", lambda v, st=st, o=o: v.scalar_tensor_tensor(st[:, o + 4:o + 5], st[:, o + 1:o + 2], 1.0 / D, st[:, o + 3:o + 4], ALU.mult, ALU.subtract), inc="p1_var")
        plan.wait("scalar", "p1_var")
        plan.run("scalar", lambda s, st=st, o=o: s.activation(st[:, o + 5:o + 6], st[:, o + 4:o + 5], AF.Sqrt, bias=ccol(1e-5)), inc="p1_sq")
        plan.wait("vector", "p1_sq")
        plan.run("vector", lambda v, st=st, o=o: v.reciprocal(st[:, o + 6:o + 7], st[:, o + 5:o + 6]))
        plan.run("vector", lambda v, xv=xv, st=st, o=o: v.tensor_scalar(_SB["p1_scr"][:], xv, st[:, o + 2:o + 3], None, ALU.subtract), inc="p1_xtfree")
        if i >= 2:
            plan.wait("vector", "p1_tp", (i - 1) * 8)
        plan.vspacer()
        plan.run("vector", lambda v, st=st, o=o, bi=bi: v.tensor_scalar(_SB["p1_xnt"][:, ts2(bi, D)], _SB["p1_scr"][:], st[:, o + 6:o + 7], None, ALU.mult), inc="p1_xn")
        plan.wait("tensor", "p1_xn")
        for j in range(8):
            n = i * 8 + j
            pj = n % 2
            if n >= 2:
                plan.wait("tensor", "p1_tev", n - 1)
            plan.run("tensor", lambda t, bi=bi, j=j, pj=pj: t.transpose(
                psb[pj][:, 0:128], _SB["p1_xnt"][:, bi * D + 128 * j: bi * D + 128 * (j + 1)], _SB["eye_s"][:]), inc="p1_tp")
            plan.wait("vector", "p1_tp")
            plan.run("vector", lambda v, i=i, j=j, pj=pj: v.tensor_copy(
                _SB["xn_fm"][:, j * N + i * 128: j * N + (i + 1) * 128], psb[pj][:, 0:128]), inc="p1_tev")

    plan.wait("sync", "p1_tev", plan.val("p1_tev"))
    for j in range(8):
        plan.dma("sync", g["xn_sp"][ts2(j, 128), :], _SB["xn_fm"][:, j * N:(j + 1) * N], "p1_sp")



def hm_off(b, h):
    return b * 256 + (h // 2) * 128 + (h % 2) * 64


def p2_weights(nc, plan, g, sbuf, psbuf):
    psbuf("Wq_s", (128, 8 * D), BF16)
    psbuf("Wk_s", (128, 8 * D), BF16)
    psbuf("Wv_s", (128, 8 * D), BF16)
    psbuf("Wf_s", (128, 8 * 256), BF16)
    psbuf("Wo_s", (128, 8 * D), BF16)
    psbuf("p2_wof", (128, D), F32)
    psbuf("p2_gnw", (128, 2), F32)
    for j in range(8):
        plan.dma("gpsimd", _SB["Wq_s"][:, ts2(j, D)], g["Wq"][ts2(j, 128), :], "w2_ld")
        plan.dma("gpsimd", _SB["Wk_s"][:, ts2(j, D)], g["Wk"][ts2(j, 128), :], "w2_ld")
        plan.dma("gpsimd", _SB["Wv_s"][:, ts2(j, D)], g["Wv"][ts2(j, 128), :], "w2_ld")
        plan.dma("gpsimd", _SB["Wf_s"][:, ts2(j, 256)], g["Wf"][ts2(j, 128), :], "w2_ld")
    plan.dma("sync", _SB["p2_gnw"][:, 0], g["gnw"][0:128], "w2_lds")
    plan.dma("sync", _SB["p2_gnw"][:, 1], g["gnw"][128:256], "w2_lds")
    for j in range(8):
        plan.dma("sync", _SB["p2_wof"][:], g["Wo"][ts2(j, 128), :], "w2_wo")
        plan.wait("vector", "w2_wo", plan.val("w2_wo"))
        plan.wait("vector", "w2_lds", 32)
        plan.run("vector", lambda v, j=j: v.tensor_scalar(
            _SB["Wo_s"][:, ts2(j, D)], _SB["p2_wof"][:], _SB["p2_gnw"][:, j % 2:j % 2 + 1], None, ALU.mult), inc="w2_wodone")
        plan.wait("sync", "w2_wodone", plan.val("w2_wodone"))
    plan.wait("tensor", "w2_ld", plan.val("w2_ld"))
    plan.wait("tensor", "w2_wodone", plan.val("w2_wodone"))


def p2_gsa(nc, plan, g, sbuf, psbuf, psf, psb, banks, debug=False):
    p2_weights(nc, plan, g, sbuf, psbuf)
    xn = _SB["xn_fm"]
    psbuf("q_fm", (128, 8 * 2 * CH), BF16)
    psbuf("k_fm", (128, 8 * 2 * CH), BF16)
    psbuf("v_tm", (128, 2 * 1024), BF16)
    psbuf("lsg", (128, 2 * 2 * CH), F32)
    psbuf("cum8", (128, 2 * 2 * CH), F32)
    psbuf("emA", (128, 2 * 2 * CH), F32)
    psbuf("emB", (128, 2 * 2 * CH), F32)
    psbuf("SAfm", (128, 2 * 2 * CH), BF16)
    psbuf("SBfm", (128, 2 * 2 * CH), BF16)
    psbuf("SAtm", (128, 512), BF16)
    psbuf("SBtm", (128, 512), BF16)
    psbuf("cumtm", (128, 512), F32)
    psbuf("Ds_tm", (128, 512), F32)
    psbuf("D_tm", (128, 512), F32)
    psbuf("K_tm", (128, 2 * 1024), BF16)
    psbuf("LG", (128, 512), F32)
    psbuf("PT", (128, 64), BF16)
    psbuf("PTT", (128, 1024), BF16)
    psbuf("sst", (128, 48), F32)
    psbuf("esc", (128, 256), F32)
    psbuf("osc", (128, 256), F32)
    psbuf("osc2", (128, 256), F32)
    psbuf("Atr_s", (128, 128), BF16)
    psbuf("Btr_s", (128, 128), BF16)
    psbuf("sgm", (128, 2 * 2 * CH), F32)
    psbuf("dbg_atr_s", (128, 128), BF16)
    psbuf("dbg_hkt_s", (128, 128), BF16)
    psbuf("dbg_lgp_s", (128, 64), F32)
    psbuf("o_n", (128, 2 * 1024), BF16)
    psbuf("o_fm", (128, 2 * 1024), BF16)
    psbuf("HKa", (128, 1024), BF16)
    psbuf("HKb2", (128, 1024), BF16)
    psbuf("HVa", (128, 1024), BF16)
    psbuf("HVb2", (128, 1024), BF16)
    psbuf("HKT", (128, 1024), BF16)
    psbuf("s0t", (128, 2048), BF16)
    psbuf("dcol", (128, 16), F32)

    plan.run("vector", lambda v: v.memset(_SB["HKa"][:], 0.0), inc="p2_init")
    plan.run("vector", lambda v: v.memset(_SB["HVa"][:], 0.0), inc="p2_init")

    def bh_slice(tens, b, h, width=64):
        return None

    for ci in range(NCH):
        st_par = ci % 2
        HKold = _SB["HKa"] if st_par == 0 else _SB["HKb2"]
        HKnew = _SB["HKb2"] if st_par == 0 else _SB["HKa"]
        HVold = _SB["HVa"] if st_par == 0 else _SB["HVb2"]
        HVnew = _SB["HVb2"] if st_par == 0 else _SB["HVa"]

        def xn_rhs(j, ci=ci):
            a = xn[:].rearrange("p (j b t) -> p j b t", j=8, b=BPC)
            return a[:, j, :, ci * CH:(ci + 1) * CH]

        if ci == 0:
            plan.wait("tensor", "w2_ld", plan.val("w2_ld"))
            plan.wait("tensor", "w2_wodone", plan.val("w2_wodone"))
            plan.wait("tensor", "p2_init", 2)
            plan.wait("tensor", "p1_sp", plan.val("p1_sp"))
        for (wname, dest, fn) in (("Wq_s", "q_fm", AF.Silu), ("Wk_s", "k_fm", AF.Silu)):
            for jj in range(8):
                bank = f"pf{jj % 2}"
                pst = psf[jj % 2]
                mms = [
                    (lambda t, w=wname, kt=kt, jj=jj, pst=pst, xr=xn_rhs:
                     t.matmul(pst[:, 0:256], _SB[w][:, kt * D + jj * 128: kt * D + (jj + 1) * 128],
                              xr(kt), start=(kt == 0), stop=(kt == 7)))
                    for kt in range(8)
                ]
                evacs = [
                    (lambda s, dest=dest, jj=jj, pst=pst, fn=fn:
                     s.activation(_SB[dest][:, jj * 256:(jj + 1) * 256], pst[:, 0:256], fn))
                ]
                banks.use(bank, mms, evacs, evac_engine="scalar")
        for b in range(BPC):
            for half in range(2):
                bank = f"pf{2 + half}"
                pst = psf[2 + half]
                mms = [
                    (lambda t, b=b, half=half, kt=kt, pst=pst, ci=ci:
                     t.matmul(pst[:, 0:512],
                              xn[:, kt * N + b * T + ci * CH: kt * N + b * T + (ci + 1) * CH],
                              _SB["Wv_s"][:, kt * D + half * 512: kt * D + (half + 1) * 512],
                              start=(kt == 0), stop=(kt == 7)))
                    for kt in range(8)
                ]
                evacs = [
                    (lambda v, b=b, half=half, pst=pst:
                     v.tensor_copy(_SB["v_tm"][:, b * 1024 + half * 512: b * 1024 + (half + 1) * 512], pst[:, 0:512]))
                ]
                banks.use(bank, mms, evacs)
        for tt in range(2):
            bank = f"pf{4 + tt}"
            pst = psf[4 + tt]
            mms = [
                (lambda t, tt=tt, kt=kt, pst=pst, xr=xn_rhs:
                 t.matmul(pst[:, 0:256], _SB["Wf_s"][:, kt * 256 + tt * 128: kt * 256 + (tt + 1) * 128],
                          xr(kt), start=(kt == 0), stop=(kt == 7)))
                for kt in range(8)
            ]
            evacs = [
                (lambda s, tt=tt, pst=pst:
                 s.activation(_SB["sgm"][:, tt * 256:(tt + 1) * 256], pst[:, 0:256], AF.Sigmoid))
            ]
            banks.use(bank, mms, evacs, evac_engine="scalar")
        plan.run("scalar", lambda s: s.activation(_SB["lsg"][:], _SB["sgm"][:], AF.Ln), inc="p2_lsg")

        plan.wait("vector", "p2_lsg")
        for tt in range(2):
            for b in range(BPC):
                o = tt * 256 + b * CH
                plan.run("vector", lambda v, o=o: v.tensor_tensor_scan(
                    _SB["cum8"][:, o:o + CH], _SB["lsg"][:, o:o + CH], _SB["zeros_s"][:, 0:CH], 0.0, ALU.add, ALU.add))
        plan.vspacer()
        for tt in range(2):
            for b in range(BPC):
                c = tt * 2 + b
                o = tt * 256 + b * CH
                plan.run("vector", lambda v, c=c, o=o: v.tensor_scalar(
                    _SB["sst"][:, c:c + 1], _SB["cum8"][:, o + CH - 1:o + CH], 0.125, None, ALU.mult))
        plan.run("vector", lambda v: v.tensor_scalar(_SB["sst"][:, 4:8], _SB["sst"][:, 0:4], -1.0, None, ALU.mult), inc="p2_cl")
        plan.wait("scalar", "p2_cl")
        for tt in range(2):
            for b in range(BPC):
                c = tt * 2 + b
                o = tt * 256 + b * CH
                plan.run("scalar", lambda s, o=o: s.activation(
                    _SB["emA"][:, o:o + CH], _SB["cum8"][:, o:o + CH], AF.Exp, scale=-0.125))
                plan.run("scalar", lambda s, o=o, c=c: s.activation(
                    _SB["emB"][:, o:o + CH], _SB["cum8"][:, o:o + CH], AF.Exp, scale=-0.125, bias=_SB["sst"][:, c:c + 1]), inc="p2_em")
        plan.run("scalar", lambda s: s.activation(_SB["dcol"][:, 0:4], _SB["sst"][:, 0:4], AF.Exp), inc="p2_em")
        plan.wait("vector", "p2_em")
        for tt in range(2):
            for b in range(BPC):
                c = tt * 2 + b
                o = tt * 256 + b * CH
                plan.run("vector", lambda v, o=o: v.tensor_tensor(
                    _SB["SAfm"][:, o + 1:o + CH], _SB["emA"][:, o + 1:o + CH], _SB["emA"][:, o:o + CH - 1], ALU.subtract))
                plan.run("vector", lambda v, o=o: v.tensor_scalar(
                    _SB["SAfm"][:, o:o + 1], _SB["emA"][:, o:o + 1], 1.0, None, ALU.subtract))
                plan.run("vector", lambda v, o=o: v.tensor_tensor(
                    _SB["SBfm"][:, o + 1:o + CH], _SB["emB"][:, o + 1:o + CH], _SB["emB"][:, o:o + CH - 1], ALU.subtract))
                plan.run("vector", lambda v, o=o, c=c: v.tensor_scalar(
                    _SB["SBfm"][:, o:o + 1], _SB["emB"][:, o:o + 1], _SB["dcol"][:, c:c + 1], None, ALU.subtract), inc="p2_sfm")

        plan.wait("tensor", "p2_sfm", plan.val("p2_sfm"))
        tpn = 0
        for (src, dst, dty) in (("SAfm", "SAtm", "b"), ("SBfm", "SBtm", "b")):
            for tt in range(2):
                for b in range(BPC):
                    o = tt * 256 + b * CH
                    do = b * 256 + tt * 128
                    pj = tpn % 2
                    tpn += 1
                    banks.use(f"pb{pj}",
                              [lambda t, src=src, o=o, pj=pj: t.transpose(psb[pj][:, 0:128], _SB[src][:, o:o + CH], _SB["eye_s"][:])],
                              [lambda v, dst=dst, do=do, pj=pj: v.tensor_copy(_SB[dst][:, do:do + 128], psb[pj][:, 0:128])])
        if ci == 0:
            psbuf("eyef_s", (128, 128), F32)
            plan.wait("vector", "c_ld", plan.val("c_ld"))
            plan.run("vector", lambda v: v.tensor_copy(_SB["eyef_s"][:], _SB["eye_s"][:]), inc="p2_eyef")
            plan.wait("tensor", "p2_eyef", 1)
        for tt in range(2):
            for b in range(BPC):
                o = tt * 256 + b * CH
                do = b * 256 + tt * 128
                bank = f"pf{4 + (tt * 2 + b) % 2}"
                pst = psf[4 + (tt * 2 + b) % 2]
                banks.use(bank,
                          [lambda t, o=o, pst=pst: t.transpose(pst[:, 0:128], _SB["cum8"][:, o:o + CH], _SB["eyef_s"][:])],
                          [lambda v, do=do, pst=pst: v.tensor_copy(_SB["cumtm"][:, do:do + 128], pst[:, 0:128])])
        plan.wait("scalar", "pf4_ev", banks.uses.get("pf4", 0))
        plan.wait("scalar", "pf5_ev", banks.uses.get("pf5", 0))
        plan.run("scalar", lambda s: s.activation(_SB["D_tm"][:], _SB["cumtm"][:], AF.Exp, scale=0.125))
        plan.run("scalar", lambda s: s.activation(_SB["Ds_tm"][:], _SB["cumtm"][:], AF.Exp, scale=0.125, bias=ccol(-LN16)), inc="p2_dtm")
        for j in range(8):
            for b in range(BPC):
                so = j * 256 + b * CH
                do = b * 1024 + j * 128
                pj = tpn % 2
                tpn += 1
                banks.use(f"pb{pj}",
                          [lambda t, so=so, pj=pj: t.transpose(psb[pj][:, 0:128], _SB["k_fm"][:, so:so + CH], _SB["eye_s"][:])],
                          [lambda v, do=do, pj=pj: v.tensor_copy(_SB["K_tm"][:, do:do + 128], psb[pj][:, 0:128])])
        plan.wait("tensor", "pf4_ev", banks.uses.get("pf4", 0))
        for b in range(BPC):
            for h in range(H):
                po = (h % 2) * 64
                co = (b * 2 + h // 2) * 256
                for kh in range(2):
                    pj = tpn % 2
                    tpn += 1
                    do = ((b * H + h) * 2 + kh) * 64
                    banks.use(f"pb{pj}",
                              [lambda t, po=po, co=co, kh=kh, pj=pj, HKold=HKold: t.transpose(
                                  psb[pj][:, 0:64], HKold[po:po + 64, co + kh * 128: co + (kh + 1) * 128], _SB["eye_s"][po:po + 64, po:po + 64])],
                              [lambda v, do=do, pj=pj: v.tensor_copy(_SB["HKT"][:, do:do + 64], psb[pj][:, 0:64])])

        plan.wait("tensor", "p2_dtm", plan.val("p2_dtm"))
        plan.wait("tensor", "pb0_ev", banks.uses.get("pb0", 0))
        plan.wait("tensor", "pb1_ev", banks.uses.get("pb1", 0))
        plan.wait("tensor", "pf2_ev", banks.uses.get("pf2", 0))
        plan.wait("tensor", "pf3_ev", banks.uses.get("pf3", 0))
        p2_core(nc, plan, g, psf, psb, banks, ci, HKold, HKnew, HVold, HVnew)

        if debug and ci == 0:
            plan.barrier()
            for nm, buf in (("dbg_qfm", "q_fm"), ("dbg_kfm", "k_fm"), ("dbg_vtm", "v_tm"),
                            ("dbg_lsg", "lsg"), ("dbg_cum", "cum8"), ("dbg_SAfm", "SAfm"),
                            ("dbg_SAtm", "SAtm"), ("dbg_SBtm", "SBtm"), ("dbg_LG", "LG"),
                            ("dbg_on", "o_n"), ("dbg_Dtm", "D_tm"), ("dbg_PTT", "PTT"), ("dbg_emA", "emA"), ("dbg_emB", "emB"), ("dbg_atr", "dbg_atr_s"), ("dbg_hkt", "dbg_hkt_s"), ("dbg_lgp", "dbg_lgp_s")):
                plan.dma("sync", g[nm], _SB[buf][:], "dbgsp")
            plan.dma("sync", g["dbg_HK"], HKnew[:], "dbgsp")
            plan.dma("sync", g["dbg_HV"], HVnew[:], "dbgsp")
            plan.barrier()

        plan.wait("tensor", "pb1_ev", banks.uses.get("pb1", 0))
        for b in range(BPC):
            if ci > 0:
                plan.wait("vector", f"p2_s0sp{b}", 16 * ci)
            for half in range(2):
                bank = f"pf{2 + half}"
                pst = psf[2 + half]
                mms = [
                    (lambda t, b=b, half=half, j=j, pst=pst: t.matmul(
                        pst[:, 0:512],
                        _SB["o_fm"][:, b * 1024 + j * 128: b * 1024 + (j + 1) * 128],
                        _SB["Wo_s"][:, j * D + half * 512: j * D + (half + 1) * 512],
                        start=(j == 0), stop=(j == 7)))
                    for j in range(8)
                ]
                evacs = [
                    (lambda v, b=b, half=half, pst=pst: v.tensor_copy(
                        _SB["s0t"][:, b * 1024 + half * 512: b * 1024 + (half + 1) * 512], pst[:, 0:512]))
                ]
                banks.use(bank, mms, evacs)
            plan.wait("sync", "pf2_ev", banks.uses.get("pf2", 0))
            plan.wait("sync", "pf3_ev", banks.uses.get("pf3", 0))
            plan.dma("sync", g["s0_sp"][b * T + ci * CH: b * T + (ci + 1) * CH, :], _SB["s0t"][:, b * 1024: b * 1024 + 1024], f"p2_s0sp{b}")



def p2_core(nc, plan, g, psf, psb, banks, ci, HKold, HKnew, HVold, HVnew):
    for b in range(BPC):
        for h in range(H):
            kt0 = 2 * h
            hmo = hm_off(b, h)
            po = (h % 2) * 64
            co = (b * 2 + h // 2) * 256
            idx = b * H + h
            sco = 8 + idx
            sco_s = 16 + idx
            sco_r = 24 + idx
            sco_q = 32 + idx
            sco_t = 40 + idx

            banks.use("pf0",
                      [(lambda t, kt=kt, kt0=kt0, b=b: t.matmul(
                          psf[0][:, 0:128],
                          _SB["k_fm"][:, (kt0 + kt) * 256 + b * CH: (kt0 + kt) * 256 + b * CH + CH],
                          _SB["q_fm"][:, (kt0 + kt) * 256 + b * CH: (kt0 + kt) * 256 + b * CH + CH],
                          start=(kt == 0), stop=(kt == 1))) for kt in range(2)],
                      [(lambda v: v.tensor_tensor(_SB["Atr_s"][:], psf[0][:, 0:128], _SB["triu_s"][:], ALU.mult))])

            def lg_mm(t, kt=None, kt0=kt0, b=b, idx=idx, hmo=hmo):
                if kt is not None:
                    return t.matmul(psf[1][:, 0:64],
                                    _SB["q_fm"][:, (kt0 + kt) * 256 + b * CH: (kt0 + kt) * 256 + b * CH + CH],
                                    _SB["HKT"][:, (idx * 2 + kt) * 64: (idx * 2 + kt + 1) * 64],
                                    start=(kt == 0), stop=False)
                return t.matmul(psf[1][:, 0:64], _SB["Atr_s"][:],
                                _SB["SAtm"][:, hmo:hmo + 64], start=False, stop=True)
            banks.use("pf1",
                      [(lambda t, f=lg_mm: f(t, 0)), (lambda t, f=lg_mm: f(t, 1)), (lambda t, f=lg_mm: f(t, None))],
                      ([(lambda v, hmo=hmo: v.tensor_tensor(_SB["LG"][:, hmo:hmo + 64], psf[1][:, 0:64],
                                                           _SB["Ds_tm"][:, hmo:hmo + 64], ALU.mult))] +
                       ([(lambda v: v.tensor_copy(_SB["dbg_lgp_s"][:], psf[1][:, 0:64])),
                         (lambda v: v.tensor_copy(_SB["dbg_atr_s"][:], _SB["Atr_s"][:])),
                         (lambda v: v.tensor_copy(_SB["dbg_hkt_s"][:], _SB["HKT"][:, 0:128]))] if (ci == 0 and idx == 0) else [])),
                      extra_mm_wait=[("pf0_ev", banks.uses.get("pf0", 0))])

            plan.run("vector", lambda v, hmo=hmo, sco=sco: v.reduce_max(
                _SB["sst"][:, sco:sco + 1], _SB["LG"][:, hmo:hmo + 64], axis=AX, negate=True), inc="p2_nmx")
            plan.wait("scalar", "p2_nmx")
            plan.run("scalar", lambda s, hmo=hmo, sco=sco: s.activation(
                _SB["esc"][:, 0:64], _SB["LG"][:, hmo:hmo + 64], AF.Exp, bias=_SB["sst"][:, sco:sco + 1]), inc="p2_exp")
            plan.wait("vector", "p2_exp")
            plan.run("vector", lambda v, sco_s=sco_s: v.reduce_sum(
                _SB["sst"][:, sco_s:sco_s + 1], _SB["esc"][:, 0:64], axis=AX))
            plan.vspacer()
            plan.run("vector", lambda v, sco_s=sco_s, sco_r=sco_r: v.reciprocal(
                _SB["sst"][:, sco_r:sco_r + 1], _SB["sst"][:, sco_s:sco_s + 1]))
            plan.run("vector", lambda v, hmo=hmo: v.tensor_tensor(
                _SB["esc"][:, 64:128], _SB["esc"][:, 0:64], _SB["D_tm"][:, hmo:hmo + 64], ALU.mult))
            plan.vspacer()
            plan.run("vector", lambda v, sco_r=sco_r: v.tensor_scalar(
                _SB["PT"][:, 0:64], _SB["esc"][:, 64:128], _SB["sst"][:, sco_r:sco_r + 1], None, ALU.mult), inc="p2_pt")

            plan.wait("tensor", "p2_pt", plan.val("p2_pt"))
            banks.use("pb0",
                      [(lambda t: t.transpose(psb[0][0:64, 0:128], _SB["PT"][:, 0:64], _SB["eye_s"][:]))],
                      [(lambda v, po=po, idx=idx: v.tensor_copy(
                          _SB["PTT"][po:po + 64, idx * 128:(idx + 1) * 128], psb[0][0:64, 0:128]))])

            sa_tt = h // 2
            sao = sa_tt * 256 + b * CH
            plan.wait("tensor", "pb0_ev", banks.uses.get("pb0", 0))
            banks.use("pf0",
                      [(lambda t, po=po, sao=sao, idx=idx: t.matmul(
                          psf[0][:, 0:128],
                          _SB["SAfm"][po:po + 64, sao:sao + CH],
                          _SB["PTT"][po:po + 64, idx * 128:(idx + 1) * 128],
                          start=True, stop=True))],
                      [(lambda v: v.tensor_tensor(_SB["Btr_s"][:], psf[0][:, 0:128], _SB["triu_s"][:], ALU.mult))])

            banks.use("pf1",
                      [(lambda t, po=po, co=co, idx=idx, HVold=HVold: t.matmul(
                          psf[1][:, 0:256],
                          _SB["PTT"][po:po + 64, idx * 128:(idx + 1) * 128],
                          HVold[po:po + 64, co:co + 256], start=True, stop=False)),
                       (lambda t, b=b, h=h: t.matmul(
                          psf[1][:, 0:256], _SB["Btr_s"][:],
                          _SB["v_tm"][:, b * 1024 + h * 256: b * 1024 + (h + 1) * 256],
                          start=False, stop=True))],
                      [(lambda v: v.tensor_copy(_SB["osc"][:], psf[1][:, 0:256]))],
                      extra_mm_wait=[("pf0_ev", banks.uses.get("pf0", 0))])

            plan.run("vector", lambda v: v.tensor_tensor(_SB["osc2"][:], _SB["osc"][:], _SB["osc"][:], ALU.mult))
            plan.vspacer()
            plan.run("vector", lambda v, sco_q=sco_q: v.reduce_sum(
                _SB["sst"][:, sco_q:sco_q + 1], _SB["osc2"][:], axis=AX))
            plan.vspacer()
            plan.run("vector", lambda v, sco_q=sco_q: v.tensor_scalar(
                _SB["esc"][:, 128:129],
                _SB["sst"][:, sco_q:sco_q + 1], 1.0 / VD, 1e-5, ALU.mult, ALU.add), inc="p2_rq")
            plan.wait("scalar", "p2_rq")
            plan.run("scalar", lambda s: s.activation(
                _SB["esc"][:, 129:130], _SB["esc"][:, 128:129], AF.Sqrt), inc="p2_rsq")
            plan.wait("vector", "p2_rsq")
            plan.run("vector", lambda v, sco_t=sco_t: v.reciprocal(
                _SB["sst"][:, sco_t:sco_t + 1], _SB["esc"][:, 129:130]))
            plan.vspacer()
            plan.run("vector", lambda v, b=b, h=h, sco_t=sco_t: v.tensor_scalar(
                _SB["o_n"][:, b * 1024 + h * 256: b * 1024 + (h + 1) * 256],
                _SB["osc"][:], _SB["sst"][:, sco_t:sco_t + 1], None, ALU.mult), inc="p2_on")

            plan.wait("tensor", "p2_on", plan.val("p2_on"))
            for half in range(2):
                j = h * 2 + half
                banks.use("pb1",
                          [(lambda t, b=b, h=h, half=half: t.transpose(
                              psb[1][:, 0:128],
                              _SB["o_n"][:, b * 1024 + h * 256 + half * 128: b * 1024 + h * 256 + (half + 1) * 128],
                              _SB["eye_s"][:]))],
                          [(lambda v, b=b, j=j: v.tensor_copy(
                              _SB["o_fm"][:, b * 1024 + j * 128: b * 1024 + (j + 1) * 128], psb[1][:, 0:128]))])

            c = sa_tt * 2 + b
            tp = (0, po)
            banks.use("pf4",
                      [(lambda t, hmo=hmo, b=b, h=h, po=po: t.matmul(
                          psf[4][po:po + 64, 0:256], _SB["SBtm"][:, hmo:hmo + 64],
                          _SB["K_tm"][:, b * 1024 + h * 256: b * 1024 + (h + 1) * 256],
                          start=True, stop=True, tile_position=(0, po))),
                       (lambda t, hmo=hmo, b=b, h=h, po=po: t.matmul(
                          psf[4][po:po + 64, 256:512], _SB["SBtm"][:, hmo:hmo + 64],
                          _SB["v_tm"][:, b * 1024 + h * 256: b * 1024 + (h + 1) * 256],
                          start=True, stop=True, tile_position=(0, po)))],
                      [(lambda v, po=po, co=co, c=c, HKold=HKold, HKnew=HKnew: v.scalar_tensor_tensor(
                          HKnew[po:po + 64, co:co + 256], HKold[po:po + 64, co:co + 256],
                          _SB["dcol"][po:po + 64, c:c + 1], psf[4][po:po + 64, 0:256], ALU.mult, ALU.add)),
                       (lambda v, po=po, co=co, c=c, HVold=HVold, HVnew=HVnew: v.scalar_tensor_tensor(
                          HVnew[po:po + 64, co:co + 256], HVold[po:po + 64, co:co + 256],
                          _SB["dcol"][po:po + 64, c:c + 1], psf[4][po:po + 64, 256:512], ALU.mult, ALU.add))])



def ln_chain(plan, pfx, xv, out_ap, stname, o, i, eps=1e-5):
    st = _SB[stname]
    scr = _SB[pfx + "_scr"][:, 0:D]
    plan.run("vector", lambda v, xv=xv, st=st, o=o: v.reduce_sum(st[:, o:o + 1], xv, axis=AX))
    plan.run("vector", lambda v, xv=xv, scr=scr: v.tensor_tensor(scr, xv, xv, ALU.mult))
    plan.vspacer()
    plan.run("vector", lambda v, st=st, o=o, scr=scr: v.reduce_sum(st[:, o + 1:o + 2], scr, axis=AX))
    plan.run("vector", lambda v, st=st, o=o: v.tensor_scalar(st[:, o + 2:o + 3], st[:, o:o + 1], 1.0 / D, None, ALU.mult))
    plan.vspacer()
    plan.run("vector", lambda v, st=st, o=o: v.tensor_tensor(st[:, o + 3:o + 4], st[:, o + 2:o + 3], st[:, o + 2:o + 3], ALU.mult))
    plan.vspacer()
    plan.run("vector", lambda v, st=st, o=o: v.scalar_tensor_tensor(st[:, o + 4:o + 5], st[:, o + 1:o + 2], 1.0 / D, st[:, o + 3:o + 4], ALU.mult, ALU.subtract), inc=pfx + "_var")
    plan.wait("scalar", pfx + "_var")
    plan.run("scalar", lambda s, st=st, o=o: s.activation(st[:, o + 5:o + 6], st[:, o + 4:o + 5], AF.Sqrt, bias=ccol(eps)), inc=pfx + "_sq")
    plan.wait("vector", pfx + "_sq")
    plan.run("vector", lambda v, st=st, o=o: v.reciprocal(st[:, o + 6:o + 7], st[:, o + 5:o + 6]))
    plan.run("vector", lambda v, xv=xv, st=st, o=o, scr=scr: v.tensor_scalar(scr, xv, st[:, o + 2:o + 3], None, ALU.subtract))
    plan.vspacer()
    plan.run("vector", lambda v, st=st, o=o, out_ap=out_ap, scr=scr: v.tensor_scalar(out_ap, scr, st[:, o + 6:o + 7], None, ALU.mult), inc=pfx + "_done")


def load_vec_cols(plan, g, name, dest, ncols, sem):
    for j in range(ncols):
        plan.dma("sync", _SB[dest][:, j], g[name][j * 128:(j + 1) * 128], sem)


def wtrans_bf16(plan, g, banks, psb, srcname, tmpname, dstname, RT, CT, sem, scale=None, viafp8=False):
    for rt in range(RT):
        pr = rt % 2
        if rt >= 2:
            plan.wait("gpsimd", f"{tmpname}_fr", rt - 1)
        plan.dma("gpsimd", _SB[tmpname][:, ts2(pr, CT * 128)], g[srcname][ts2(rt, 128), :], f"{tmpname}_ld{pr}")
        plan.wait("vector", f"{tmpname}_ld{pr}", (rt // 2 + 1) * 16)
        if scale is not None:
            plan.run("vector", lambda v, pr=pr, scale=scale: v.tensor_scalar(
                _SB[tmpname + "b"][:, ts2(pr, CT * 128)], _SB[tmpname][:, ts2(pr, CT * 128)], scale, None, ALU.mult), inc=f"{tmpname}_c")
        elif viafp8:
            plan.run("vector", lambda v, pr=pr: v.tensor_copy(
                _SB[tmpname + "8"][:, ts2(pr, CT * 128)], _SB[tmpname][:, ts2(pr, CT * 128)]))
            plan.vspacer()
            plan.run("vector", lambda v, pr=pr: v.tensor_copy(
                _SB[tmpname + "b"][:, ts2(pr, CT * 128)], _SB[tmpname + "8"][:, ts2(pr, CT * 128)]), inc=f"{tmpname}_c")
        else:
            plan.run("vector", lambda v, pr=pr: v.tensor_copy(
                _SB[tmpname + "b"][:, ts2(pr, CT * 128)], _SB[tmpname][:, ts2(pr, CT * 128)]), inc=f"{tmpname}_c")
        plan.run("vector", lambda v: v.tensor_copy(_SB["vspc"][:, 0:8], _SB["vspc"][:, 8:16]), inc=f"{tmpname}_fr")
        plan.wait("tensor", f"{tmpname}_c", plan.val(f"{tmpname}_c"))
        for ct in range(CT):
            pj = (rt * CT + ct) % 2
            banks.use(f"pb{pj}",
                      [(lambda t, pr=pr, ct=ct, pj=pj: t.transpose(
                          psb[pj][:, 0:128], _SB[tmpname + "b"][:, pr * CT * 128 + ct * 128: pr * CT * 128 + (ct + 1) * 128], _SB["eye_s"][:]))],
                      [(lambda v, rt=rt, ct=ct, pj=pj: v.tensor_copy(
                          _SB[dstname][:, ct * RT * 128 + rt * 128: ct * RT * 128 + (rt + 1) * 128], psb[pj][:, 0:128]))])
    plan.run("vector", lambda v: v.tensor_copy(_SB["vspc"][:, 0:8], _SB["vspc"][:, 8:16]), inc=f"{dstname}_rdy")
    plan.wait("tensor", f"{dstname}_rdy", plan.val(f"{dstname}_rdy"))



def p3a_ffn0(nc, plan, g, sbuf, psbuf, psf, psb, banks):
    CT = 8
    psbuf("wg_t", (128, 2 * D), BF16)
    psbuf("wg_tb", (128, 2 * D), BF16)
    psbuf("wgq", (128, 8 * HID0), BF16)
    psbuf("wuq", (128, 8 * HID0), BF16)
    for si, (srcname, dstname) in enumerate((("wg", "wgq"), ("wu", "wuq"))):
        for rt in range(HID0 // 128):
            rtg = si * (HID0 // 128) + rt
            pr = rtg % 2
            if rtg >= 2:
                plan.wait("gpsimd", "p3w_fr", rtg - 1)
            plan.dma("gpsimd", _SB["wg_t"][:, ts2(pr, D)], g[srcname][ts2(rt, 128), :], f"p3w_ld{pr}")
            plan.wait("vector", f"p3w_ld{pr}", (rtg // 2 + 1) * 16)
            plan.run("vector", lambda v, pr=pr: v.tensor_scalar(
                _SB["wg_tb"][:, ts2(pr, D)], _SB["wg_t"][:, ts2(pr, D)], 1.0 / 127.0, None, ALU.mult), inc="p3w_c")
            plan.run("vector", lambda v: v.tensor_copy(_SB["vspc"][:, 0:8], _SB["vspc"][:, 8:16]), inc="p3w_fr")
            plan.wait("tensor", "p3w_c", plan.val("p3w_c"))
            for ct in range(CT):
                pj = (rt * CT + ct) % 2
                banks.use(f"pb{pj}",
                          [(lambda t, pr=pr, ct=ct, pj=pj: t.transpose(
                              psb[pj][:, 0:128], _SB["wg_tb"][:, pr * D + ct * 128: pr * D + (ct + 1) * 128], _SB["eye_s"][:]))],
                          [(lambda v, rt=rt, ct=ct, pj=pj, dstname=dstname: v.tensor_copy(
                              _SB[dstname][:, ct * HID0 + rt * 128: ct * HID0 + (rt + 1) * 128], psb[pj][:, 0:128]))])
    plan.run("vector", lambda v: v.tensor_copy(_SB["vspc"][:, 0:8], _SB["vspc"][:, 8:16]), inc="p3w_rdy")
    plan.wait("tensor", "p3w_rdy", 1)

    psbuf("p3_xn", (128, 2 * 8 * 512), BF16)
    psbuf("p3_gsc", (128, 2 * 512), F32)
    psbuf("p3_h", (128, 2 * 512), BF16)
    NT3 = N // 512
    for it in range(NT3):
        pr = it % 2
        if it >= 2:
            plan.wait("sync", "p3_xnfree", (it - 1) * 24)
        for j in range(8):
            plan.dma("sync", _SB["p3_xn"][:, pr * 4096 + j * 512:(pr * 4096) + (j + 1) * 512],
                     g["xn_sp"][ts2(j, 128), it * 512:(it + 1) * 512], f"p3_xnld{pr}")
        plan.wait("tensor", f"p3_xnld{pr}", (it // 2 + 1) * 16 * 8)
        for co in range(HID0 // 128):
            cb = co % 2
            hb = co % 2
            plan.wait("scalar", f"pf{2 + cb}_ev", plan.val(f"pf{2 + cb}_ev"))
            banks.use(f"pf{cb}",
                      [(lambda t, j=j, co=co, pr=pr, cb=cb: t.matmul(
                          psf[cb][:, 0:512], _SB["wgq"][:, j * HID0 + co * 128: j * HID0 + (co + 1) * 128],
                          _SB["p3_xn"][:, pr * 4096 + j * 512: pr * 4096 + (j + 1) * 512],
                          start=(j == 0), stop=(j == 7))) for j in range(8)],
                      [(lambda s, cb=cb: s.activation(_SB["p3_gsc"][:, ts2(cb, 512)], psf[cb][:, 0:512], AF.Silu))],
                      evac_engine="scalar")
            plan.wait("vector", f"p3_hsp{hb}", plan.val(f"p3_hsp{hb}"))
            plan.wait("vector", f"pf{cb}_ev", plan.val(f"pf{cb}_ev"))
            banks.use(f"pf{2 + cb}",
                      [(lambda t, j=j, co=co, pr=pr, cb=cb: t.matmul(
                          psf[2 + cb][:, 0:512], _SB["wuq"][:, j * HID0 + co * 128: j * HID0 + (co + 1) * 128],
                          _SB["p3_xn"][:, pr * 4096 + j * 512: pr * 4096 + (j + 1) * 512],
                          start=(j == 0), stop=(j == 7))) for j in range(8)],
                      [(lambda v, cb=cb, hb=hb: v.tensor_tensor(
                          _SB["p3_h"][:, ts2(hb, 512)], _SB["p3_gsc"][:, ts2(cb, 512)], psf[2 + cb][:, 0:512], ALU.mult))],
                      extra_mm_wait=None)
            if co == HID0 // 128 - 1:
                plan.run("tensor", lambda t: t.nop(), inc="p3_xnfree", inck=24)
            plan.wait("sync", f"pf{2 + cb}_ev", banks.uses.get(f"pf{2 + cb}", 0))
            plan.dma("sync", g["h_sp"][ts2(co, 128), it * 512:(it + 1) * 512], _SB["p3_h"][:, ts2(hb, 512)], f"p3_hsp{hb}")



def p3b_ffn0(nc, plan, g, sbuf, psbuf, psf, psb, banks):
    psbuf("wd_t", (128, 2 * HID0), BF16)
    psbuf("wd_tb", (128, 2 * HID0), BF16)
    psbuf("wdq", (128, 24 * D), BF16)
    for rt in range(D // 128):
        pr = rt % 2
        if rt >= 2:
            plan.wait("gpsimd", "p3bw_fr", rt - 1)
        plan.dma("gpsimd", _SB["wd_t"][:, ts2(pr, HID0)], g["wd"][ts2(rt, 128), :], f"p3bw_ld{pr}")
        plan.wait("vector", f"p3bw_ld{pr}", (rt // 2 + 1) * 16)
        plan.run("vector", lambda v, pr=pr: v.tensor_scalar(
            _SB["wd_tb"][:, ts2(pr, HID0)], _SB["wd_t"][:, ts2(pr, HID0)], 1.0 / 127.0, None, ALU.mult), inc="p3bw_c")
        plan.run("vector", lambda v: v.tensor_copy(_SB["vspc"][:, 0:8], _SB["vspc"][:, 8:16]), inc="p3bw_fr")
        plan.wait("tensor", "p3bw_c", plan.val("p3bw_c"))
        for ct in range(24):
            pj = (rt * 24 + ct) % 2
            banks.use(f"pb{pj}",
                      [(lambda t, pr=pr, ct=ct, pj=pj: t.transpose(
                          psb[pj][:, 0:128], _SB["wd_tb"][:, pr * HID0 + ct * 128: pr * HID0 + (ct + 1) * 128], _SB["eye_s"][:]))],
                      [(lambda v, rt=rt, ct=ct, pj=pj: v.tensor_copy(
                          _SB["wdq"][:, ct * D + rt * 128: ct * D + (rt + 1) * 128], psb[pj][:, 0:128]))])
    plan.run("vector", lambda v: v.tensor_copy(_SB["vspc"][:, 0:8], _SB["vspc"][:, 8:16]), inc="p3b_rdy")
    plan.wait("tensor", "p3b_rdy", 1)

    psbuf("p3b_h", (128, 2 * 24 * 128), BF16)
    psbuf("p3b_x", (128, 2 * D), F32)
    psbuf("p3b_s0", (128, 2 * D), BF16)
    psbuf("p3b_g0", (128, D), F32)
    psbuf("p3b_x1", (128, D), F32)
    psbuf("p3b_xn1", (128, D), F32)
    psbuf("p3b_scr", (128, D), F32)
    psbuf("p3b_st", (128, 16), F32)
    NT = N // 128
    for it in range(NT):
        pr = it % 2
        if it >= 2:
            plan.wait("sync", "p3b_free", it - 1)
        for co in range(24):
            plan.dma("sync", _SB["p3b_h"][:, pr * 3072 + co * 128: pr * 3072 + (co + 1) * 128],
                     g["h_sp"][ts2(co, 128), it * 128:(it + 1) * 128], f"p3b_hld{pr}")
        plan.dma("sync", _SB["p3b_x"][:, ts2(pr, D)], g["x"][ts2(it, 128), :], f"p3b_xld{pr}")
        plan.dma("sync", _SB["p3b_s0"][:, ts2(pr, D)], g["s0_sp"][ts2(it, 128), :], f"p3b_s0ld{pr}")
        plan.wait("tensor", f"p3b_hld{pr}", (it // 2 + 1) * 16 * 24)
        for half in range(2):
            banks.use(f"pf{half}",
                      [(lambda t, co=co, half=half, pr=pr: t.matmul(
                          psf[half][:, 0:512],
                          _SB["p3b_h"][:, pr * 3072 + co * 128: pr * 3072 + (co + 1) * 128],
                          _SB["wdq"][:, co * D + half * 512: co * D + (half + 1) * 512],
                          start=(co == 0), stop=(co == 23))) for co in range(24)],
                      [(lambda s, half=half: s.activation(_SB["p3b_g0"][:, ts2(half, 512)], psf[half][:, 0:512], AF.Silu))],
                      evac_engine="scalar")
        plan.wait("vector", "pf0_ev", banks.uses.get("pf0", 0))
        plan.wait("vector", "pf1_ev", banks.uses.get("pf1", 0))
        plan.wait("vector", f"p3b_xld{pr}", (it // 2 + 1) * 16)
        plan.wait("vector", f"p3b_s0ld{pr}", (it // 2 + 1) * 16)
        plan.run("vector", lambda v, pr=pr: v.tensor_tensor(_SB["p3b_scr"][:], _SB["p3b_s0"][:, ts2(pr, D)], _SB["p3b_g0"][:], ALU.mult))
        plan.vspacer()
        plan.run("vector", lambda v, pr=pr: v.tensor_tensor(_SB["p3b_x1"][:], _SB["p3b_x"][:, ts2(pr, D)], _SB["p3b_scr"][:], ALU.add), inc="p3b_free")
        plan.vspacer()
        ln_chain(plan, "p3b", _SB["p3b_x1"][:], _SB["p3b_xn1"][:], "p3b_st", 8 * (it % 2), it)
        plan.wait("sync", "p3b_done", plan.val("p3b_done"))
        plan.dma("sync", g["x1_sp"][ts2(it, 128), :], _SB["p3b_x1"][:], "p3b_x1sp")
        plan.dma("sync", g["xn1_sp"][ts2(it, 128), :], _SB["p3b_xn1"][:], "p3b_xn1sp")
        plan.wait("vector", "p3b_x1sp", plan.val("p3b_x1sp"))
        plan.wait("vector", "p3b_xn1sp", plan.val("p3b_xn1sp"))



def p4_wkv(nc, plan, g, sbuf, psbuf, psf, psb, banks):
    for (src, dst) in (("Wk_t", "wktq"), ("Wv_t", "wvtq"), ("Wr_t", "wrtq")):
        psbuf(dst, (128, 8 * D), BF16)
    _wsc = ExitStack()
    _SB["p4_wt"] = _wsc.enter_context(nc.sbuf_tensor("p4_wt", [128, 2 * D], BF16))
    _SB["p4_wtb"] = _wsc.enter_context(nc.sbuf_tensor("p4_wtb", [128, 2 * D], BF16))
    for si, (src, dst) in enumerate((("Wk_t", "wktq"), ("Wv_t", "wvtq"), ("Wr_t", "wrtq"))):
        for rt in range(8):
            rtg = si * 8 + rt
            pr = rtg % 2
            if rtg >= 2:
                plan.wait("gpsimd", "p4w_fr", rtg - 1)
            plan.dma("gpsimd", _SB["p4_wt"][:, ts2(pr, D)], g[src][ts2(rt, 128), :], f"p4w_ld{pr}")
            plan.wait("vector", f"p4w_ld{pr}", (rtg // 2 + 1) * 16)
            plan.run("vector", lambda v, pr=pr: v.tensor_copy(_SB["p4_wtb"][:, ts2(pr, D)], _SB["p4_wt"][:, ts2(pr, D)]), inc="p4w_c")
            plan.run("vector", lambda v: v.tensor_copy(_SB["vspc"][:, 0:8], _SB["vspc"][:, 8:16]), inc="p4w_fr")
            plan.wait("tensor", "p4w_c", plan.val("p4w_c"))
            for ct in range(8):
                pj = (rt * 8 + ct) % 2
                banks.use(f"pb{pj}",
                          [(lambda t, pr=pr, ct=ct, pj=pj: t.transpose(
                              psb[pj][:, 0:128], _SB["p4_wtb"][:, pr * D + ct * 128: pr * D + (ct + 1) * 128], _SB["eye_s"][:]))],
                          [(lambda v, rt=rt, ct=ct, pj=pj, dst=dst: v.tensor_copy(
                              _SB[dst][:, ct * D + rt * 128: ct * D + (rt + 1) * 128], psb[pj][:, 0:128]))])
    plan.run("vector", lambda v: v.tensor_copy(_SB["vspc"][:, 0:8], _SB["vspc"][:, 8:16]), inc="p4w_rdy")
    plan.wait("tensor", "p4w_rdy", 1)
    plan.barrier()
    _wsc.close()
    for nm, dst in (("ln1_g", "p4_g"), ("ln1_b", "p4_b"), ("mix_k", "p4_mk"),
                    ("mix_v", "p4_mv"), ("mix_r", "p4_mr"), ("t_first", "p4_u")):
        psbuf(dst, (128, 8), F32)
        load_vec_cols(plan, g, nm, dst, 8, "p4_vld")
    psbuf("p4_rho", (128, 24), F32)
    plan.wait("vector", "p4_vld", plan.val("p4_vld"))
    for mi, mname in enumerate(("p4_mk", "p4_mv", "p4_mr")):
        plan.run("vector", lambda v, mi=mi, mname=mname: v.tensor_scalar(
            _SB["p4_rho"][:, mi * 8:(mi + 1) * 8], _SB[mname][:], 1.0, None, ALU.subtract), inc="p4_rho_d")
    psbuf("eyef4", (128, 128), F32)
    plan.run("vector", lambda v: v.tensor_copy(_SB["eyef4"][:], _SB["eye_s"][:]), inc="p4_eyef")
    plan.wait("tensor", "p4_eyef", 1)

    W = WKV_W
    NCOLS = 16 * W
    psbuf("p4_xn1", (128, 2 * D), F32)
    psbuf("p4_h1n", (128, 2 * D), F32)

    psbuf("p4_st", (128, 16), F32)
    psbuf("p4_hfm", (128, NCOLS), F32)
    psbuf("p4_xn1fm", (128, NCOLS), F32)
    psbuf("p4_d", (128, NCOLS), F32)
    psbuf("p4_tsk", (128, NCOLS), BF16)
    psbuf("p4_tsv", (128, NCOLS), BF16)
    psbuf("p4_tsr", (128, NCOLS), BF16)
    psbuf("p4_hcar", (128, 16), F32)
    psbuf("p4_E", (128, NCOLS), F32)
    psbuf("p4_EU", (128, NCOLS), F32)
    psbuf("p4_V", (128, NCOLS), F32)
    _SB["p4_scr"] = _SB["p4_V"]
    psbuf("p4_R", (128, NCOLS), F32)
    psbuf("p4_W0", (128, NCOLS), F32)
    psbuf("p4_W1", (128, NCOLS), F32)
    psbuf("p4_Wp", (128, NCOLS), F32)
    psbuf("p4_M", (128, NCOLS), F32)
    psbuf("p4_c", (128, NCOLS), F32)
    _SB["p4_cR"] = _SB["p4_hfm"]
    _SB["p4_q"] = _SB["p4_W0"]
    _SB["p4_qp"] = _SB["p4_E"]
    psbuf("p4_s1", (128, NCOLS), F32)
    psbuf("p4_s2", (128, NCOLS), F32)
    psbuf("p4_r", (128, NCOLS), F32)
    psbuf("p4_Wcar", (128, 16), F32)
    psbuf("p4_qcar", (128, 16), F32)
    _SB["p4_xa"] = _SB["p4_d"]
    plan.run("vector", lambda v: v.memset(_SB["p4_Wcar"][:], 0.0), inc="p4_init")
    plan.run("vector", lambda v: v.memset(_SB["p4_qcar"][:], 0.0), inc="p4_init")
    plan.run("vector", lambda v: v.memset(_SB["p4_hcar"][:], 0.0), inc="p4_init")
    plan.wait("vector", "p4_rho_d", plan.val("p4_rho_d"))
    plan.wait("vector", "p4_init", plan.val("p4_init"))

    for w in range(NWIN):
        if w >= 1:
            plan.wait("sync", "p4_xn1free", w)
            plan.wait("sync", "p4_xasp", plan.val("p4_xasp"))
        for b in range(BPC):
            plan.dma("sync", _SB["p4_xn1"][:, ts2(b, D)], g["xn1_sp"][b * T + w * W: b * T + (w + 1) * W, :], "p4_xl")
        plan.wait("vector", "p4_xl", plan.val("p4_xl"))
        for b in range(BPC):
            ln_chain(plan, "p4", _SB["p4_xn1"][:, ts2(b, D)], _SB["p4_h1n"][:, ts2(b, D)], "p4_st", 8 * b, w)
        plan.wait("tensor", "p4_done", plan.val("p4_done"))
        tpn = 0
        for (srcn, dstn) in (("p4_h1n", "p4_hfm"), ("p4_xn1", "p4_xn1fm")):
            for b in range(BPC):
                for j in range(8):
                    pk = 4 + tpn % 2
                    tpn += 1
                    banks.use(f"pf{pk}",
                              [(lambda t, srcn=srcn, b=b, j=j, pk=pk: t.transpose(
                                  psf[pk][:, 0:128], _SB[srcn][:, b * D + j * 128: b * D + (j + 1) * 128], _SB["eyef4"][:]))],
                              [(lambda v, dstn=dstn, b=b, j=j, pk=pk: v.tensor_copy(
                                  _SB[dstn][:, j * 256 + b * 128: j * 256 + b * 128 + 128], psf[pk][:, 0:128]))])
        plan.wait("vector", "pf4_ev", banks.uses.get("pf4", 0))
        plan.wait("vector", "pf5_ev", banks.uses.get("pf5", 0))
        for j in range(8):
            plan.run("vector", lambda v, j=j: v.tensor_scalar(
                _SB["p4_hfm"][:, j * 256:(j + 1) * 256], _SB["p4_hfm"][:, j * 256:(j + 1) * 256],
                _SB["p4_g"][:, j:j + 1], _SB["p4_b"][:, j:j + 1], ALU.mult, ALU.add), inc="p4_aff")
        plan.vspacer()
        for j in range(8):
            for b in range(BPC):
                o = j * 256 + b * 128
                c = j * 2 + b
                plan.run("vector", lambda v, o=o: v.tensor_tensor(
                    _SB["p4_d"][:, o + 1:o + W], _SB["p4_hfm"][:, o + 1:o + W], _SB["p4_hfm"][:, o:o + W - 1], ALU.subtract))
                plan.run("vector", lambda v, o=o, c=c: v.tensor_scalar(
                    _SB["p4_d"][:, o:o + 1], _SB["p4_hfm"][:, o:o + 1], _SB["p4_hcar"][:, c:c + 1], None, ALU.subtract))
        plan.vspacer()
        plan.run("vector", lambda v: v.tensor_copy(
            _SB["p4_hcar"][:], _SB["p4_hfm"][:].rearrange("p (c t) -> p t c", t=W)[:, W - 1, :]), inc="p4_hcar_d")
        for mi, dstn in enumerate(("p4_tsk", "p4_tsv", "p4_tsr")):
            for j in range(8):
                plan.run("vector", lambda v, mi=mi, j=j, dstn=dstn: v.scalar_tensor_tensor(
                    _SB[dstn][:, j * 256:(j + 1) * 256], _SB["p4_d"][:, j * 256:(j + 1) * 256],
                    _SB["p4_rho"][:, mi * 8 + j: mi * 8 + j + 1], _SB["p4_hfm"][:, j * 256:(j + 1) * 256],
                    ALU.mult, ALU.add), inc="p4_ts")
        plan.wait("tensor", "p4_ts", plan.val("p4_ts"))
        for jj in range(8):
            cb = jj % 2
            banks.use(f"pf{cb}",
                      [(lambda t, j=j, jj=jj, cb=cb: t.matmul(
                          psf[cb][:, 0:256], _SB["wktq"][:, j * D + jj * 128: j * D + (jj + 1) * 128],
                          _SB["p4_tsk"][:, j * 256:(j + 1) * 256], start=(j == 0), stop=(j == 7))) for j in range(8)],
                      [(lambda s, jj=jj, cb=cb: s.activation(_SB["p4_E"][:, jj * 256:(jj + 1) * 256], psf[cb][:, 0:256], AF.Exp)),
                       (lambda s, jj=jj, cb=cb: s.activation(_SB["p4_EU"][:, jj * 256:(jj + 1) * 256], psf[cb][:, 0:256], AF.Exp, bias=_SB["p4_u"][:, jj:jj + 1]))],
                      evac_engine="scalar")
            banks.use(f"pf{2 + cb}",
                      [(lambda t, j=j, jj=jj, cb=cb: t.matmul(
                          psf[2 + cb][:, 0:256], _SB["wvtq"][:, j * D + jj * 128: j * D + (jj + 1) * 128],
                          _SB["p4_tsv"][:, j * 256:(j + 1) * 256], start=(j == 0), stop=(j == 7))) for j in range(8)],
                      [(lambda v, jj=jj, cb=cb: v.tensor_copy(_SB["p4_V"][:, jj * 256:(jj + 1) * 256], psf[2 + cb][:, 0:256]))])
            banks.use(f"pf{4 + cb}",
                      [(lambda t, j=j, jj=jj, cb=cb: t.matmul(
                          psf[4 + cb][:, 0:256], _SB["wrtq"][:, j * D + jj * 128: j * D + (jj + 1) * 128],
                          _SB["p4_tsr"][:, j * 256:(j + 1) * 256], start=(j == 0), stop=(j == 7))) for j in range(8)],
                      [(lambda s, jj=jj, cb=cb: s.activation(_SB["p4_r"][:, jj * 256:(jj + 1) * 256], psf[4 + cb][:, 0:256], AF.Sigmoid))],
                      evac_engine="scalar")
        plan.wait("vector", "pf0_ev", banks.uses.get("pf0", 0))
        plan.wait("vector", "pf1_ev", banks.uses.get("pf1", 0))
        plan.wait("vector", "pf2_ev", banks.uses.get("pf2", 0))
        plan.wait("vector", "pf3_ev", banks.uses.get("pf3", 0))
        plan.run("vector", lambda v: v.tensor_tensor(_SB["p4_R"][:], _SB["p4_E"][:], _SB["p4_V"][:], ALU.mult))
        plan.vspacer()
        for blk in range(16):
            o = blk * W
            plan.run("vector", lambda v, o=o, blk=blk: v.tensor_tensor_scan(
                _SB["p4_W0"][:, o:o + W], _SB["p4_E"][:, o:o + W], _SB["zeros_s"][:, 0:W], _SB["p4_Wcar"][:, blk:blk + 1], ALU.add, ALU.add))
        plan.vspacer()
        plan.run("vector", lambda v: v.tensor_copy(_SB["p4_Wp"][:, 1:NCOLS], _SB["p4_W0"][:, 0:NCOLS - 1]))
        plan.vspacer()
        plan.run("vector", lambda v: v.tensor_copy(
            _SB["p4_Wp"][:].rearrange("p (c t) -> p t c", t=W)[:, 0, :], _SB["p4_Wcar"][:]))
        plan.vspacer()
        plan.run("vector", lambda v: v.tensor_tensor(_SB["p4_M"][:], _SB["p4_Wp"][:], _SB["p4_E"][:], ALU.max))
        plan.vspacer()
        plan.run("vector", lambda v: v.scalar_tensor_tensor(_SB["p4_s1"][:], _SB["p4_M"][:], EPS_WKV, _SB["p4_E"][:], ALU.mult, ALU.add))
        plan.vspacer()
        for blk in range(16):
            o = blk * W
            plan.run("vector", lambda v, o=o, blk=blk: v.tensor_tensor_scan(
                _SB["p4_W1"][:, o:o + W], _SB["p4_s1"][:, o:o + W], _SB["zeros_s"][:, 0:W], _SB["p4_Wcar"][:, blk:blk + 1], ALU.add, ALU.add))
        plan.vspacer()
        plan.run("vector", lambda v: v.tensor_copy(_SB["p4_Wp"][:, 1:NCOLS], _SB["p4_W1"][:, 0:NCOLS - 1]))
        plan.vspacer()
        plan.run("vector", lambda v: v.tensor_copy(
            _SB["p4_Wp"][:].rearrange("p (c t) -> p t c", t=W)[:, 0, :], _SB["p4_Wcar"][:]))
        plan.vspacer()
        plan.run("vector", lambda v: v.tensor_copy(
            _SB["p4_Wcar"][:], _SB["p4_W1"][:].rearrange("p (c t) -> p t c", t=W)[:, W - 1, :]))
        plan.vspacer()
        plan.run("vector", lambda v: v.tensor_tensor(_SB["p4_M"][:], _SB["p4_Wp"][:], _SB["p4_E"][:], ALU.max), inc="p4_m_d")
        plan.wait("scalar", "p4_m_d", plan.val("p4_m_d"))
        plan.run("scalar", lambda s: s.activation(_SB["p4_s2"][:], _SB["p4_M"][:], AF.Ln))
        plan.run("scalar", lambda s: s.activation(_SB["warm_s"][:, 4:8], _SB["warm_s"][:, 0:4], AF.Exp))
        plan.run("scalar", lambda s: s.activation(_SB["p4_s2"][:], _SB["p4_s2"][:], AF.Exp, scale=-1.0), inc="p4_im_d")
        plan.wait("vector", "p4_im_d")
        plan.run("vector", lambda v: v.tensor_tensor(_SB["p4_c"][:], _SB["p4_W1"][:], _SB["p4_s2"][:], ALU.mult))
        plan.vspacer()
        plan.run("vector", lambda v: v.tensor_tensor(_SB["p4_cR"][:], _SB["p4_c"][:], _SB["p4_R"][:], ALU.mult))
        plan.vspacer()
        for blk in range(16):
            o = blk * W
            plan.run("vector", lambda v, o=o, blk=blk: v.tensor_tensor_scan(
                _SB["p4_q"][:, o:o + W], _SB["p4_c"][:, o:o + W], _SB["p4_cR"][:, o:o + W], _SB["p4_qcar"][:, blk:blk + 1], ALU.mult, ALU.add))
        plan.vspacer()
        plan.run("vector", lambda v: v.tensor_copy(_SB["p4_qp"][:, 1:NCOLS], _SB["p4_q"][:, 0:NCOLS - 1]))
        plan.vspacer()
        plan.run("vector", lambda v: v.tensor_copy(
            _SB["p4_qp"][:].rearrange("p (c t) -> p t c", t=W)[:, 0, :], _SB["p4_qcar"][:]))
        plan.vspacer()
        plan.run("vector", lambda v: v.tensor_copy(
            _SB["p4_qcar"][:], _SB["p4_q"][:].rearrange("p (c t) -> p t c", t=W)[:, W - 1, :]))
        plan.vspacer()
        plan.run("vector", lambda v: v.tensor_tensor(_SB["p4_s1"][:], _SB["p4_EU"][:], _SB["p4_V"][:], ALU.mult))
        plan.vspacer()
        plan.run("vector", lambda v: v.tensor_tensor(_SB["p4_s1"][:], _SB["p4_qp"][:], _SB["p4_s1"][:], ALU.add))
        plan.vspacer()
        plan.run("vector", lambda v: v.tensor_tensor(_SB["p4_M"][:], _SB["p4_Wp"][:], _SB["p4_EU"][:], ALU.max))
        plan.vspacer()
        plan.run("vector", lambda v: v.tensor_tensor(_SB["p4_c"][:], _SB["p4_Wp"][:], _SB["p4_EU"][:], ALU.add))
        plan.vspacer()
        plan.run("vector", lambda v: v.scalar_tensor_tensor(_SB["p4_s2"][:], _SB["p4_M"][:], EPS_WKV, _SB["p4_c"][:], ALU.mult, ALU.add), inc="p4_den_d")
        plan.wait("scalar", "p4_den_d")
        plan.run("scalar", lambda s: s.activation(_SB["p4_s2"][:], _SB["p4_s2"][:], AF.Ln))
        plan.run("scalar", lambda s: s.activation(_SB["warm_s"][:, 4:8], _SB["warm_s"][:, 0:4], AF.Exp))
        plan.run("scalar", lambda s: s.activation(_SB["p4_s2"][:], _SB["p4_s2"][:], AF.Exp, scale=-1.0), inc="p4_iden_d")
        plan.wait("vector", "p4_iden_d")
        plan.run("vector", lambda v: v.tensor_tensor(_SB["p4_s1"][:], _SB["p4_s1"][:], _SB["p4_s2"][:], ALU.mult))
        plan.vspacer()
        plan.run("vector", lambda v: v.tensor_tensor(_SB["p4_s1"][:], _SB["p4_r"][:], _SB["p4_s1"][:], ALU.mult))
        plan.vspacer()
        plan.run("vector", lambda v: v.tensor_tensor(_SB["p4_xa"][:], _SB["p4_xn1fm"][:], _SB["p4_s1"][:], ALU.add), inc="p4_xa_d")
        plan.wait("tensor", "p4_xa_d", plan.val("p4_xa_d"))
        plan.wait("vector", "p4_xasp", plan.val("p4_xasp"))
        for b in range(BPC):
            for j in range(8):
                pk = 4 + (b * 8 + j) % 2
                banks.use(f"pf{pk}",
                          [(lambda t, b=b, j=j, pk=pk: t.transpose(
                              psf[pk][:, 0:128], _SB["p4_xa"][:, j * 256 + b * 128: j * 256 + b * 128 + 128], _SB["eyef4"][:]))],
                          [(lambda v, b=b, j=j, pk=pk: v.tensor_copy(
                              _SB["p4_xn1"][:, b * D + j * 128: b * D + (j + 1) * 128], psf[pk][:, 0:128]))])
        plan.run("vector", lambda v: v.tensor_copy(_SB["vspc"][:, 0:8], _SB["vspc"][:, 8:16]), inc="p4_xn1free")
        plan.wait("sync", "pf4_ev", banks.uses.get("pf4", 0))
        plan.wait("sync", "pf5_ev", banks.uses.get("pf5", 0))
        for b in range(BPC):
            plan.dma("sync", g["xa_sp"][b * T + w * W: b * T + (w + 1) * W, :], _SB["p4_xn1"][:, ts2(b, D)], "p4_xasp")



def p5_ffn1(nc, plan, g, sbuf, psbuf, psf, psb, banks):
    psbuf("w1q", (128, 8 * HID1), BF16)
    psbuf("w2q", (128, 32 * D), BF16)
    _wsc = ExitStack()
    _SB["p5_wt"] = _wsc.enter_context(nc.sbuf_tensor("p5_wt", [128, 2 * HID1], F32))
    _SB["p5_wt8"] = _wsc.enter_context(nc.sbuf_tensor("p5_wt8", [128, 2 * HID1], FP8))
    _SB["p5_wtb"] = _wsc.enter_context(nc.sbuf_tensor("p5_wtb", [128, 2 * HID1], BF16))
    rtg = 0
    for (src, dst, RT, CT) in (("w1", "w1q", HID1 // 128, 8), ("w2", "w2q", D // 128, 32)):
        for rt in range(RT):
            pr = rtg % 2
            ncol = CT * 128
            if rtg >= 2:
                plan.wait("sync", "p5w_fr", rtg - 1)
            plan.dma("sync", _SB["p5_wt"][:, ts2(pr, HID1)][:, 0:ncol], g[src][ts2(rt, 128), :], f"p5w_ld{pr}")
            plan.wait("vector", f"p5w_ld{pr}", (rtg // 2 + 1) * 16)
            plan.run("vector", lambda v, pr=pr, ncol=ncol: v.tensor_copy(
                _SB["p5_wt8"][:, ts2(pr, HID1)][:, 0:ncol], _SB["p5_wt"][:, ts2(pr, HID1)][:, 0:ncol]))
            plan.vspacer()
            plan.run("vector", lambda v, pr=pr, ncol=ncol: v.tensor_copy(
                _SB["p5_wtb"][:, ts2(pr, HID1)][:, 0:ncol], _SB["p5_wt8"][:, ts2(pr, HID1)][:, 0:ncol]), inc="p5w_c")
            plan.run("vector", lambda v: v.tensor_copy(_SB["vspc"][:, 0:8], _SB["vspc"][:, 8:16]), inc="p5w_fr")
            plan.wait("tensor", "p5w_c", plan.val("p5w_c"))
            for ct in range(CT):
                pj = (rtg * CT + ct) % 2
                banks.use(f"pb{pj}",
                          [(lambda t, pr=pr, ct=ct, pj=pj: t.transpose(
                              psb[pj][:, 0:128], _SB["p5_wtb"][:, pr * HID1 + ct * 128: pr * HID1 + (ct + 1) * 128], _SB["eye_s"][:]))],
                          [(lambda v, rt=rt, ct=ct, pj=pj, dst=dst, RT=RT: v.tensor_copy(
                              _SB[dst][:, ct * RT * 128 + rt * 128: ct * RT * 128 + (rt + 1) * 128], psb[pj][:, 0:128]))])
            rtg += 1
    plan.run("vector", lambda v: v.tensor_copy(_SB["vspc"][:, 0:8], _SB["vspc"][:, 8:16]), inc="p5w_rdy")
    plan.wait("tensor", "p5w_rdy", 1)
    plan.barrier()
    _wsc.close()
    psbuf("p5_b1", (128, 32), F32)
    psbuf("p5_b2", (128, 8), F32)
    psbuf("p5_b8", (128, 32), FP8)
    load_vec_cols(plan, g, "b1", "p5_b1", 32, "p5_bld")
    load_vec_cols(plan, g, "b2", "p5_b2", 8, "p5_bld")
    plan.wait("vector", "p5_bld", plan.val("p5_bld"))
    plan.run("vector", lambda v: v.tensor_copy(_SB["p5_b8"][:, 0:32], _SB["p5_b1"][:]))
    plan.vspacer()
    plan.run("vector", lambda v: v.tensor_copy(_SB["p5_b1"][:], _SB["p5_b8"][:, 0:32]))
    plan.vspacer()
    plan.run("vector", lambda v: v.tensor_copy(_SB["p5_b8"][:, 0:8], _SB["p5_b2"][:]))
    plan.vspacer()
    plan.run("vector", lambda v: v.tensor_copy(_SB["p5_b2"][:], _SB["p5_b8"][:, 0:8]), inc="p5_b_d")
    plan.wait("scalar", "p5_b_d", plan.val("p5_b_d"))

    TT5 = 256
    psbuf("p5_x", (128, 2 * D), F32)
    psbuf("p5_x8", (128, D), FP8)
    psbuf("p5_xb", (128, 2 * D), BF16)
    psbuf("p5_xfm", (128, 8 * TT5), BF16)
    psbuf("p5_h16", (128, 2 * TT5), FP16)
    psbuf("p5_h8", (128, 2 * TT5), FP8)
    psbuf("p5_h4", (128, 32 * TT5), BF16)
    psbuf("p5_y", (128, 2 * TT5), F32)
    psbuf("p5_m", (128, 2 * TT5), FP16)
    psbuf("p5_g1", (128, 2 * TT5), F32)
    psbuf("p5_g1c", (128, 2 * TT5), BF16)
    psbuf("p5_g1b", (128, 2 * D), BF16)
    NT5 = N // TT5
    for it in range(NT5):
        if it >= 1:
            plan.wait("sync", "p5_xfree", it)
        for sub in range(2):
            plan.dma("sync", _SB["p5_x"][:, ts2(sub, D)], g["xn1_sp"][it * TT5 + sub * 128: it * TT5 + (sub + 1) * 128, :], "p5_xl")
        plan.wait("vector", "p5_xl", plan.val("p5_xl"))
        for sub in range(2):
            plan.run("vector", lambda v, sub=sub: v.tensor_copy(_SB["p5_x8"][:], _SB["p5_x"][:, ts2(sub, D)]))
            plan.vspacer()
            plan.run("vector", lambda v, sub=sub: v.tensor_copy(_SB["p5_xb"][:, ts2(sub, D)], _SB["p5_x8"][:]), inc="p5_xq")
            plan.vspacer()
        plan.run("vector", lambda v: v.tensor_copy(_SB["vspc"][:, 0:8], _SB["vspc"][:, 8:16]), inc="p5_xfree")
        plan.wait("tensor", "p5_xq", plan.val("p5_xq"))
        for sub in range(2):
            for j in range(8):
                pj = (sub * 8 + j) % 2
                banks.use(f"pb{pj}",
                          [(lambda t, sub=sub, j=j, pj=pj: t.transpose(
                              psb[pj][:, 0:128], _SB["p5_xb"][:, sub * D + j * 128: sub * D + (j + 1) * 128], _SB["eye_s"][:]))],
                          [(lambda v, sub=sub, j=j, pj=pj: v.tensor_copy(
                              _SB["p5_xfm"][:, j * TT5 + sub * 128: j * TT5 + sub * 128 + 128], psb[pj][:, 0:128]))])
        plan.wait("tensor", "pb0_ev", banks.uses.get("pb0", 0))
        plan.wait("tensor", "pb1_ev", banks.uses.get("pb1", 0))
        for co in range(32):
            cb = co % 2
            if co >= 2:
                plan.wait("scalar", "p5_h_d", plan.val("p5_h_d") - 1)
            banks.use(f"pf{cb}",
                      [(lambda t, j=j, co=co, cb=cb: t.matmul(
                          psf[cb][:, 0:TT5], _SB["w1q"][:, j * HID1 + co * 128: j * HID1 + (co + 1) * 128],
                          _SB["p5_xfm"][:, j * TT5:(j + 1) * TT5], start=(j == 0), stop=(j == 7))) for j in range(8)],
                      [(lambda s, co=co, cb=cb: s.activation(_SB["p5_h16"][:, ts2(cb, TT5)], psf[cb][:, 0:TT5], AF.Identity, bias=_SB["p5_b1"][:, co:co + 1]))],
                      evac_engine="scalar")
            plan.wait("vector", f"pf{cb}_ev", banks.uses.get(f"pf{cb}", 0))
            plan.run("vector", lambda v, cb=cb: v.tensor_scalar(_SB["p5_h8"][:, ts2(cb, TT5)], _SB["p5_h16"][:, ts2(cb, TT5)], 0.0, None, ALU.max))
            plan.vspacer()
            plan.run("vector", lambda v, co=co, cb=cb: v.tensor_copy(_SB["p5_h4"][:, ts2(co, TT5)], _SB["p5_h8"][:, ts2(cb, TT5)]), inc="p5_h_d")
        plan.wait("tensor", "p5_h_d", plan.val("p5_h_d"))
        for od in range(8):
            cb = 4 + od % 2
            mp = od % 2
            if od >= 2:
                plan.wait("sync", "p5_y_d", plan.val("p5_y_d") - 1)
            plan.dma("sync", _SB["p5_m"][:, ts2(mp, TT5)], g["mask_fm"][ts2(od, 128), it * TT5:(it + 1) * TT5], f"p5_ml{mp}")
            if od >= 2:
                plan.wait("scalar", "p5_y_d", plan.val("p5_y_d") - 1)
            banks.use(f"pf{cb}",
                      [(lambda t, co=co, od=od, cb=cb: t.matmul(
                          psf[cb][:, 0:TT5], _SB["w2q"][:, co * D + od * 128: co * D + (od + 1) * 128],
                          _SB["p5_h4"][:, ts2(co, TT5)], start=(co == 0), stop=(co == 31))) for co in range(32)],
                      [(lambda s, od=od, cb=cb, mp=mp: s.activation(_SB["p5_h16"][:, ts2(mp, TT5)], psf[cb][:, 0:TT5], AF.Identity, bias=_SB["p5_b2"][:, od:od + 1]))],
                      evac_engine="scalar")
            plan.wait("vector", f"pf{cb}_ev", banks.uses.get(f"pf{cb}", 0))
            plan.wait("vector", f"p5_ml{mp}", plan.val(f"p5_ml{mp}"))
            plan.run("vector", lambda v, mp=mp: v.tensor_tensor(_SB["p5_y"][:, ts2(mp, TT5)], _SB["p5_h16"][:, ts2(mp, TT5)], _SB["p5_m"][:, ts2(mp, TT5)], ALU.mult), inc="p5_y_d")
            plan.wait("scalar", "p5_y_d", plan.val("p5_y_d"))
            plan.run("scalar", lambda s, mp=mp: s.activation(_SB["p5_g1"][:, ts2(mp, TT5)], _SB["p5_y"][:, ts2(mp, TT5)], AF.Silu), inc="p5_sil")
            plan.wait("vector", "p5_sil", plan.val("p5_sil"))
            plan.run("vector", lambda v, mp=mp: v.tensor_copy(_SB["p5_g1c"][:, ts2(mp, TT5)], _SB["p5_g1"][:, ts2(mp, TT5)]), inc="p5_g1_d")
            plan.wait("tensor", "p5_g1_d", plan.val("p5_g1_d"))
            for sub in range(2):
                pj = (od * 2 + sub) % 2
                banks.use(f"pb{pj}",
                          [(lambda t, sub=sub, pj=pj, mp=mp: t.transpose(
                              psb[pj][:, 0:128], _SB["p5_g1c"][:, mp * TT5 + sub * 128: mp * TT5 + (sub + 1) * 128], _SB["eye_s"][:]))],
                          [(lambda v, od=od, sub=sub, pj=pj: v.tensor_copy(
                              _SB["p5_g1b"][:, sub * D + od * 128: sub * D + (od + 1) * 128], psb[pj][:, 0:128]))])
        plan.wait("sync", "pb0_ev", banks.uses.get("pb0", 0))
        plan.wait("sync", "pb1_ev", banks.uses.get("pb1", 0))
        for sub in range(2):
            plan.dma("sync", g["g1_sp"][it * TT5 + sub * 128: it * TT5 + (sub + 1) * 128, :], _SB["p5_g1b"][:, ts2(sub, D)], "p5_gsp")
        plan.wait("vector", "p5_gsp", plan.val("p5_gsp"))



def p6_cmix(nc, plan, g, sbuf, psbuf, psf, psb, banks):
    psbuf("wkcq", (128, 8 * HID1), BF16)
    psbuf("wvcq", (128, 32 * D), BF16)
    psbuf("wrcq", (128, 8 * D), BF16)
    _wsc = ExitStack()
    _SB["p6_wt"] = _wsc.enter_context(nc.sbuf_tensor("p6_wt", [128, 2 * HID1], BF16))
    _SB["p6_wtb"] = _wsc.enter_context(nc.sbuf_tensor("p6_wtb", [128, 2 * HID1], BF16))
    rtg = 0
    for (src, dst, RT, CT) in (("Wk_c", "wkcq", HID1 // 128, 8), ("Wv_c", "wvcq", D // 128, 32), ("Wr_c", "wrcq", D // 128, 8)):
        for rt in range(RT):
            pr = rtg % 2
            ncol = CT * 128
            if rtg >= 2:
                plan.wait("gpsimd", "p6w_fr", rtg - 1)
            plan.dma("gpsimd", _SB["p6_wt"][:, ts2(pr, HID1)][:, 0:ncol], g[src][ts2(rt, 128), :], f"p6w_ld{pr}")
            plan.wait("vector", f"p6w_ld{pr}", (rtg // 2 + 1) * 16)
            plan.run("vector", lambda v, pr=pr, ncol=ncol: v.tensor_copy(
                _SB["p6_wtb"][:, ts2(pr, HID1)][:, 0:ncol], _SB["p6_wt"][:, ts2(pr, HID1)][:, 0:ncol]), inc="p6w_c")
            plan.run("vector", lambda v: v.tensor_copy(_SB["vspc"][:, 0:8], _SB["vspc"][:, 8:16]), inc="p6w_fr")
            plan.wait("tensor", "p6w_c", plan.val("p6w_c"))
            for ct in range(CT):
                pj = (rtg * CT + ct) % 2
                banks.use(f"pb{pj}",
                          [(lambda t, pr=pr, ct=ct, pj=pj: t.transpose(
                              psb[pj][:, 0:128], _SB["p6_wtb"][:, pr * HID1 + ct * 128: pr * HID1 + (ct + 1) * 128], _SB["eye_s"][:]))],
                          [(lambda v, rt=rt, ct=ct, pj=pj, dst=dst, RT=RT: v.tensor_copy(
                              _SB[dst][:, ct * RT * 128 + rt * 128: ct * RT * 128 + (rt + 1) * 128], psb[pj][:, 0:128]))])
            rtg += 1
    plan.run("vector", lambda v: v.tensor_copy(_SB["vspc"][:, 0:8], _SB["vspc"][:, 8:16]), inc="p6w_rdy")
    plan.wait("tensor", "p6w_rdy", 1)
    plan.barrier()
    _wsc.close()
    for nm, dst in (("ln2_g", "p6_g"), ("ln2_b", "p6_b"), ("cmix_k", "p6_mk"), ("cmix_r", "p6_mr")):
        psbuf(dst, (128, 8), F32)
        load_vec_cols(plan, g, nm, dst, 8, "p6_vld")
    psbuf("p6_rho", (128, 16), F32)
    plan.wait("vector", "p6_vld", plan.val("p6_vld"))
    for mi, mname in enumerate(("p6_mk", "p6_mr")):
        plan.run("vector", lambda v, mi=mi, mname=mname: v.tensor_scalar(
            _SB["p6_rho"][:, mi * 8:(mi + 1) * 8], _SB[mname][:], 1.0, None, ALU.subtract), inc="p6_rho_d")
    psbuf("eyef6", (128, 128), F32)
    plan.run("vector", lambda v: v.tensor_copy(_SB["eyef6"][:], _SB["eye_s"][:]), inc="p6_eyef")
    plan.wait("tensor", "p6_eyef", 1)
    plan.wait("vector", "p6_rho_d", plan.val("p6_rho_d"))

    psbuf("p6_xa", (128, 2 * D), F32)
    psbuf("p6_h2n", (128, D), F32)
    psbuf("p6_scr", (128, D), F32)
    psbuf("p6_st", (128, 16), F32)
    psbuf("p6_hfm", (128, 8 * 128), F32)
    psbuf("p6_d", (128, 8 * 128), F32)
    psbuf("p6_tsk", (128, 8 * 128), BF16)
    psbuf("p6_tsr", (128, 8 * 128), BF16)
    psbuf("p6_hcar", (128, 8), F32)
    psbuf("p6_sq", (128, 32 * 128), BF16)
    psbuf("p6_krelu", (128, 2 * 128), F32)
    psbuf("p6_vv", (128, D), F32)
    psbuf("p6_rc", (128, D), F32)
    psbuf("p6_x1", (128, D), F32)
    psbuf("p6_g1", (128, D), BF16)
    psbuf("p6_out", (128, D), F32)
    plan.run("vector", lambda v: v.memset(_SB["p6_hcar"][:], 0.0), inc="p6_init")
    plan.wait("vector", "p6_init", 1)
    NT = N // 128
    for it in range(NT):
        pr = it % 2
        if it >= 2:
            plan.wait("sync", "p6_xafree", it - 1)
        plan.dma("sync", _SB["p6_xa"][:, ts2(pr, D)], g["xa_sp"][ts2(it, 128), :], f"p6_xal{pr}")
        plan.wait("vector", f"p6_xal{pr}", (it // 2 + 1) * 16)
        ln_chain(plan, "p6", _SB["p6_xa"][:, ts2(pr, D)], _SB["p6_h2n"][:], "p6_st", 8 * (it % 2), it)
        plan.wait("tensor", "p6_done", plan.val("p6_done"))
        for j in range(8):
            pk = 4 + j % 2
            banks.use(f"pf{pk}",
                      [(lambda t, j=j, pk=pk: t.transpose(
                          psf[pk][:, 0:128], _SB["p6_h2n"][:, ts2(j, 128)], _SB["eyef6"][:]))],
                      [(lambda v, j=j, pk=pk: v.tensor_copy(_SB["p6_hfm"][:, ts2(j, 128)], psf[pk][:, 0:128]))])
        plan.wait("vector", "pf4_ev", banks.uses.get("pf4", 0))
        plan.wait("vector", "pf5_ev", banks.uses.get("pf5", 0))
        for j in range(8):
            plan.run("vector", lambda v, j=j: v.tensor_scalar(
                _SB["p6_hfm"][:, ts2(j, 128)], _SB["p6_hfm"][:, ts2(j, 128)],
                _SB["p6_g"][:, j:j + 1], _SB["p6_b"][:, j:j + 1], ALU.mult, ALU.add), inc="p6_aff")
        plan.vspacer()
        for j in range(8):
            o = j * 128
            plan.run("vector", lambda v, o=o: v.tensor_tensor(
                _SB["p6_d"][:, o + 1:o + 128], _SB["p6_hfm"][:, o + 1:o + 128], _SB["p6_hfm"][:, o:o + 127], ALU.subtract))
            if it % (T // 128) == 0:
                plan.run("vector", lambda v, o=o, j=j: v.tensor_copy(_SB["p6_d"][:, o:o + 1], _SB["p6_hfm"][:, o:o + 1]))
            else:
                plan.run("vector", lambda v, o=o, j=j: v.tensor_scalar(
                    _SB["p6_d"][:, o:o + 1], _SB["p6_hfm"][:, o:o + 1], _SB["p6_hcar"][:, j:j + 1], None, ALU.subtract))
        plan.vspacer()
        plan.run("vector", lambda v: v.tensor_copy(
            _SB["p6_hcar"][:], _SB["p6_hfm"][:].rearrange("p (j t) -> p t j", t=128)[:, 127, :]), inc="p6_hcar_d")
        for mi, dstn in enumerate(("p6_tsk", "p6_tsr")):
            for j in range(8):
                plan.run("vector", lambda v, mi=mi, j=j, dstn=dstn: v.scalar_tensor_tensor(
                    _SB[dstn][:, ts2(j, 128)], _SB["p6_d"][:, ts2(j, 128)],
                    _SB["p6_rho"][:, mi * 8 + j: mi * 8 + j + 1], _SB["p6_hfm"][:, ts2(j, 128)],
                    ALU.mult, ALU.add), inc="p6_ts")
        plan.wait("tensor", "p6_ts", plan.val("p6_ts"))
        for co in range(32):
            cb = co % 2
            banks.use(f"pf{cb}",
                      [(lambda t, j=j, co=co, cb=cb: t.matmul(
                          psf[cb][:, 0:128], _SB["wkcq"][:, j * HID1 + co * 128: j * HID1 + (co + 1) * 128],
                          _SB["p6_tsk"][:, ts2(j, 128)], start=(j == 0), stop=(j == 7))) for j in range(8)],
                      [(lambda s, co=co, cb=cb: s.activation(_SB["p6_krelu"][:, ts2(cb, 128)], psf[cb][:, 0:128], AF.Relu))],
                      evac_engine="scalar")
            plan.wait("vector", f"pf{cb}_ev", banks.uses.get(f"pf{cb}", 0))
            plan.run("vector", lambda v, co=co, cb=cb: v.tensor_tensor(
                _SB["p6_sq"][:, ts2(co, 128)], _SB["p6_krelu"][:, ts2(cb, 128)], _SB["p6_krelu"][:, ts2(cb, 128)], ALU.mult), inc="p6_sqv")
            plan.wait("scalar", "p6_sqv", plan.val("p6_sqv") - 1 if co < 31 else plan.val("p6_sqv"))
        plan.wait("tensor", "p6_sqv", plan.val("p6_sqv"))
        for half in range(2):
            cb = 2 + half
            banks.use(f"pf{cb}",
                      [(lambda t, co=co, half=half, cb=cb: t.matmul(
                          psf[cb][:, 0:512], _SB["p6_sq"][:, ts2(co, 128)],
                          _SB["wvcq"][:, co * D + half * 512: co * D + (half + 1) * 512],
                          start=(co == 0), stop=(co == 31))) for co in range(32)],
                      [(lambda v, half=half, cb=cb: v.tensor_copy(_SB["p6_vv"][:, ts2(half, 512)], psf[cb][:, 0:512]))])
        for half in range(2):
            cb = 4 + half
            banks.use(f"pf{cb}",
                      [(lambda t, j=j, half=half, cb=cb: t.matmul(
                          psf[cb][:, 0:512], _SB["p6_tsr"][:, ts2(j, 128)],
                          _SB["wrcq"][:, j * D + half * 512: j * D + (half + 1) * 512],
                          start=(j == 0), stop=(j == 7))) for j in range(8)],
                      [(lambda s, half=half, cb=cb: s.activation(_SB["p6_rc"][:, ts2(half, 512)], psf[cb][:, 0:512], AF.Sigmoid))],
                      evac_engine="scalar")
        plan.wait("vector", "pf2_ev", banks.uses.get("pf2", 0))
        plan.wait("vector", "pf3_ev", banks.uses.get("pf3", 0))
        plan.wait("vector", "pf4_ev", banks.uses.get("pf4", 0))
        plan.wait("vector", "pf5_ev", banks.uses.get("pf5", 0))
        if it >= 1:
            plan.wait("sync", "p6_out_d", it)
        plan.dma("sync", _SB["p6_x1"][:], g["x1_sp"][ts2(it, 128), :], "p6_x1l")
        plan.dma("sync", _SB["p6_g1"][:], g["g1_sp"][ts2(it, 128), :], "p6_g1l")
        plan.run("vector", lambda v: v.tensor_tensor(_SB["p6_scr"][:], _SB["p6_rc"][:], _SB["p6_vv"][:], ALU.mult))
        plan.vspacer()
        plan.run("vector", lambda v, pr=pr: v.tensor_tensor(_SB["p6_scr"][:], _SB["p6_xa"][:, ts2(pr, D)], _SB["p6_scr"][:], ALU.add), inc="p6_xafree")
        plan.wait("vector", "p6_x1l", plan.val("p6_x1l"))
        plan.wait("vector", "p6_g1l", plan.val("p6_g1l"))
        plan.run("vector", lambda v: v.tensor_tensor(_SB["p6_out"][:], _SB["p6_scr"][:], _SB["p6_g1"][:], ALU.mult))
        plan.vspacer()
        plan.run("vector", lambda v: v.tensor_tensor(_SB["p6_out"][:], _SB["p6_x1"][:], _SB["p6_out"][:], ALU.add), inc="p6_out_d")
        plan.wait("sync", "p6_out_d", plan.val("p6_out_d"))
        plan.dma("sync", g["out"][ts2(it, 128), :], _SB["p6_out"][:], "p6_osp")
        plan.wait("vector", "p6_osp", plan.val("p6_osp"))



_WMAP = [
    ("Wq", "L0_attn_Wq"), ("Wk", "L0_attn_Wk"), ("Wv", "L0_attn_Wv"),
    ("Wf", "L0_attn_Wf"), ("gnw", "L0_attn_g_norm_w"), ("Wo", "L0_attn_Wo"),
    ("wg", "L0_ffn_w_gate"), ("wu", "L0_ffn_w_up"), ("wd", "L0_ffn_w_down"),
    ("t_first", "L1_attn_time_first"),
    ("Wk_t", "L1_attn_Wk_t"), ("Wv_t", "L1_attn_Wv_t"), ("Wr_t", "L1_attn_Wr_t"),
    ("Wk_c", "L1_attn_Wk_c"), ("Wv_c", "L1_attn_Wv_c"), ("Wr_c", "L1_attn_Wr_c"),
    ("ln1_g", "L1_attn_ln1_g"), ("ln1_b", "L1_attn_ln1_b"),
    ("ln2_g", "L1_attn_ln2_g"), ("ln2_b", "L1_attn_ln2_b"),
    ("mix_k", "L1_attn_mix_k"), ("mix_v", "L1_attn_mix_v"), ("mix_r", "L1_attn_mix_r"),
    ("cmix_k", "L1_attn_cmix_k"), ("cmix_r", "L1_attn_cmix_r"),
    ("w1", "L1_ffn_w1"), ("b1", "L1_ffn_b1"), ("w2", "L1_ffn_w2"), ("b2", "L1_ffn_b2"),
]


def _host_constants():
    import ml_dtypes
    eye = np.eye(128, dtype=ml_dtypes.bfloat16)
    triu = np.triu(np.ones((128, 128), np.float32)).astype(ml_dtypes.bfloat16)
    import jax
    cpu = jax.devices("cpu")[0]
    with jax.default_device(cpu):
        keep = np.asarray(jax.random.bernoulli(jax.random.key(42), 0.8, (B, T, D)))
    return eye, triu, keep


def make_in_maps(inputs, debug=False):
    eye, triu, keep = _host_constants()
    x_full = np.ascontiguousarray(np.asarray(inputs["L0_attn_x"], dtype=np.float32))
    base = {}
    for local, ref in _WMAP:
        base[local] = np.ascontiguousarray(np.asarray(inputs[ref]))
    in_maps = []
    for c in range(NCORES):
        m = dict(base)
        xs = x_full[c * BPC:(c + 1) * BPC]
        m["x"] = np.ascontiguousarray(xs.reshape(N, D))
        km = keep[c * BPC:(c + 1) * BPC].astype(np.float16) * np.float16(1.25)
        m["mask_fm"] = np.ascontiguousarray(km.reshape(N, D).T)
        m["eye"] = eye
        m["triu"] = triu
        in_maps.append(m)
    return in_maps


def kernel(**inputs):
    nc = build_program(debug=False)
    in_maps = make_in_maps(inputs)
    res = run_bass_kernel_spmd(nc, in_maps, core_ids=list(range(NCORES)))
    out = np.concatenate([r["out"].reshape(BPC, T, D) for r in res.results], axis=0)
    return out.astype(np.float32)


if __name__ == "__main__":
    ncp = build_program()
    print("built, instructions:", ncp.n_instructions())
